# version 07 (final)
# baseline (speedup 1.0000x reference)
import math
import numpy as np
import ml_dtypes
import concourse.bass as bass
import concourse.mybir as mybir
from concourse.bass_utils import run_bass_kernel_spmd

F32 = mybir.dt.float32
BF16 = mybir.dt.bfloat16
AF = mybir.ActivationFunctionType
ALU = mybir.AluOpType
BFDT = ml_dtypes.bfloat16

DM = 4096
NCH = 32
TOK = 2048
SEQ = 4096
NT = 4
TS = 512
NSUB = 4
KSTAGE = 99
SERIAL_ATT = 0
RG = [[0, 1], [2, 3], [4, 5], [6, 7]]
SHIFT = 16.0
EPS = 1e-5

ATTN_A = dict(name="a", H=32, dh=128, dhc=1, kvw=1024, QB=256, nslot=8,
              scale=1.0 / math.sqrt(128.0), G=4, nmask=4, hper=4, half=64)
ATTN_B = dict(name="b", H=16, dh=256, dhc=2, kvw=4096, QB=512, nslot=4,
              scale=1.0 / 16.0, G=2, nmask=8, hper=1, half=32)

ENGINES = ("sync", "scalar", "vector", "tensor", "gpsimd")


class Sem:
    __slots__ = ("h", "n", "name")
    def __init__(self, h, name):
        self.h, self.n, self.name = h, 0, name


class StopEmission(Exception):
    pass


class Bld:
    def __init__(self):
        self.q = {e: [] for e in ENGINES}

    def emit(self, e, fn):
        self.q[e].append(fn)

    def wait(self, e, sem, val=None):
        v = sem.n if val is None else val
        if v <= 0:
            return
        assert v <= sem.n, f"wait beyond emitted incs on {sem.name}: {v} > {sem.n}"
        self.emit(e, lambda en, _h=sem.h, _v=v: en.wait_ge(_h, _v))

    def dma(self, e, out, in_, sem, transpose=False):
        def f(en, _o=out, _i=in_, _h=sem.h, _t=transpose):
            en.dma_start(out=_o, in_=_i, transpose=_t).then_inc(_h, 16)
        self.emit(e, f)
        sem.n += 16

    def op(self, e, fn, sem=None, inc=1):
        if sem is None:
            self.emit(e, fn)
        else:
            def g(en, _f=fn, _h=sem.h, _k=inc):
                _f(en).then_inc(_h, _k)
            self.emit(e, g)
            sem.n += inc


def tile_weight(W):
    din, dout = W.shape
    P = dout // 128
    return np.ascontiguousarray(
        W.reshape(NCH, 128, P, 128).transpose(2, 1, 0, 3)).astype(np.float32)


def rope_perm_cols(Wt):
    W = Wt.copy()
    perm64 = np.concatenate([np.arange(0, 64, 2), np.arange(1, 64, 2)])
    for h in range(16):
        base = h * 256
        W[:, base:base + 64] = Wt[:, base + perm64]
    return W


def core_blocks(core):
    sigma = core % 2
    return [j for j in range(16) if j % 2 == 1 - sigma]


def core_rows(core):
    s = core // 2
    return np.concatenate([np.arange(s * SEQ + j * 256, s * SEQ + (j + 1) * 256)
                           for j in core_blocks(core)])


def make_masks(core):
    blocks = core_blocks(core)
    out = {}
    for spec in (ATTN_A, ATTN_B):
        nslot, nmask, QB = spec["nslot"], spec["nmask"], spec["QB"]
        ngroups = nmask // 4
        m = np.zeros((nslot, nmask, 128, QB), dtype=np.float32)
        for k in range(nslot):
            prefix = (SEQ // nslot) * (k + 1)
            njq = QB // 256
            qglob = np.concatenate(
                [np.arange(256) + 256 * blocks[k * njq + t] for t in range(njq)])
            j2_last = prefix // 512 - ngroups
            for g in range(ngroups):
                j2 = j2_last + g
                for bb in range(2):
                    jblk = 2 * j2 + (1 - bb)
                    for h in range(2):
                        g0 = 256 * jblk + 128 * h
                        kk = g0 + np.arange(128)
                        m[k, g * 4 + bb * 2 + h] = (kk[:, None] <= qglob[None, :])
        out[spec["name"]] = m.astype(BFDT)
    return out


def make_rope_tables(core):
    pos = (core_rows(core) % SEQ).astype(np.float64)
    inv_a = 1.0 / (10000.0 ** (np.arange(64) / 64.0))
    inv_b = 1.0 / (10000.0 ** (np.arange(0, 64, 2) / 64.0))
    ang_a = pos[None, :] * inv_a[:, None]
    ang_b = pos[None, :] * inv_b[:, None]
    return (np.cos(ang_a).astype(np.float32), np.sin(ang_a).astype(np.float32),
            np.cos(ang_b).astype(np.float32), np.sin(ang_b).astype(np.float32))


def build_program():
    import contextlib
    nc = bass.Bass("TRN2", target_bir_lowering=False, detect_race_conditions=False)

    xin = nc.dram_tensor("xin", [NCH, 128, TOK], F32, kind="ExternalInput").ap()
    w = {}
    for nm, P in (("wq_a", 32), ("wk_a", 8), ("wv_a", 8), ("wo_a", 32),
                  ("wq_b", 32), ("wk_b", 32), ("wv_b", 32), ("wo_b", 32),
                  ("wkf", 32)):
        w[nm] = nc.dram_tensor(nm, [P, 128, NCH, 128], F32, kind="ExternalInput").ap()
    la = nc.dram_tensor("la", [16, 32, DM], F32, kind="ExternalInput").ap()
    lb = nc.dram_tensor("lb", [16, 128, 32, 32], F32, kind="ExternalInput").ap()
    las = nc.dram_tensor("las", [32, DM], F32, kind="ExternalInput").ap()
    lbs = nc.dram_tensor("lbs", [128, 32, 32], F32, kind="ExternalInput").ap()
    biasc_d = nc.dram_tensor("biasc", [128, 32], F32, kind="ExternalInput").ap()
    gate2 = nc.dram_tensor("gate2", [1, 32], F32, kind="ExternalInput").ap()
    ropeA_d = nc.dram_tensor("ropeA", [128, TOK], F32, kind="ExternalInput").ap()
    ropeB_d = nc.dram_tensor("ropeB", [64, TOK], F32, kind="ExternalInput").ap()
    mka = nc.dram_tensor("mka", [8, 4, 128, 256], BF16, kind="ExternalInput").ap()
    mkb = nc.dram_tensor("mkb", [4, 8, 128, 512], BF16, kind="ExternalInput").ap()
    idf_d = nc.dram_tensor("idf", [128, 128], F32, kind="ExternalInput").ap()
    y = nc.dram_tensor("y", [NCH, 128, TOK], F32, kind="ExternalOutput").ap()

    xnext = nc.dram_tensor("xnext", [NCH, 128, TOK], BF16).ap()
    xspill = nc.dram_tensor("xspill", [NCH, 128, TOK], BF16).ap()
    qf = nc.dram_tensor("qfm", [NCH, 128, TOK], BF16).ap()
    off_d = nc.dram_tensor("offm", [NCH, 128, TOK], BF16).ap()
    kf = {"a": nc.dram_tensor("kfa", [1024, TOK], BF16).ap(),
          "b": nc.dram_tensor("kfb", [4096, TOK], BF16).ap()}
    vf = {"a": nc.dram_tensor("vfa", [8, 128, TOK], BF16).ap(),
          "b": nc.dram_tensor("vfb", [32, 128, TOK], BF16).ap()}
    vt = {"a": nc.dram_tensor("vta", [TOK, 1024], BF16).ap(),
          "b": nc.dram_tensor("vtb", [TOK, 4096], BF16).ap()}
    kfag = {"a": nc.dram_tensor("kfa_ag", [2048, TOK], BF16).ap(),
            "b": nc.dram_tensor("kfb_ag", [8192, TOK], BF16).ap()}
    vtag = {"a": nc.dram_tensor("vta_ag", [2 * TOK, 1024], BF16).ap(),
            "b": nc.dram_tensor("vtb_ag", [2 * TOK, 4096], BF16).ap()}

    b = Bld()
    ctx = contextlib.ExitStack()
    sb = lambda nm, shape, dt: ctx.enter_context(nc.sbuf_tensor(nm, shape, dt))
    x_sb = sb("x_sb", [128, NCH * TOK], BF16)
    wpan = sb("wpan", [128, 2 * 4096], BF16)
    stgf = sb("stgf", [128, 2 * TS], F32)
    stgb = sb("stgb", [128, 4 * TS], BF16)
    stat_m = sb("stat_m", [128, TOK], F32)
    rir = sb("rir", [128, 2 * TOK], F32)
    sqb = sb("sqb", [128, TOK], BF16)
    ones_sb = sb("ones_sb", [128, 128], BF16)
    idf = sb("idf_s", [128, 128], F32)
    biasc = sb("biasc_s", [128, 32], F32)
    gsb = sb("gsb", [1, 32], F32)
    gbc = sb("gbc", [128, 32], F32)
    cst = sb("cst", [128, 2], F32)
    a_stk = sb("a_stk", [128, 2 * NCH * 64], BF16)
    b_stk = sb("b_stk", [64, 2 * NCH * 128], BF16)
    ax_sb = sqb[0:64, :]
    ps = ctx.enter_context(nc.psum_tensor("ps", [128, 4096], F32))

    sems = {}
    def S(name):
        sems[name] = Sem(ctx.enter_context(nc.semaphore(name)), name)
        return sems[name]
    s_xld = S("s_xld"); s_wld = S("s_wld"); s_mmg = S("s_mmg"); s_evd = S("s_evd")
    s_std = S("s_std"); s_sq = S("s_sq"); s_sqf = S("s_sqf"); s_stm = S("s_stm")
    s_var = S("s_var"); s_rst = S("s_rst"); s_ctr = S("s_ctr"); s_spl = S("s_spl")
    s_vb = S("s_vb"); s_vbs = S("s_vbs"); s_ag = S("s_ag"); s_q = S("s_q")
    s_kv = S("s_kv"); s_srg = S("s_srg"); s_exp = S("s_exp"); s_pt = S("s_pt")
    s_pvr = S("s_pvr"); s_rc = S("s_rc"); s_rb = S("s_rb"); s_rbe = S("s_rbe")
    s_oev = S("s_oev"); s_ost = S("s_ost"); s_mk = S("s_mk"); s_xrr = S("s_xrr")
    s_lord = S("s_lord"); s_lort = S("s_lort"); s_lorv = S("s_lorv")
    s_msc = S("s_msc"); s_yst = S("s_yst"); s_vv = S("s_vv"); s_fld = S("s_fld")
    s_kkp = [S("s_kk0"), S("s_kk1")]
    s_ofl = S("s_ofl"); s_tbl = S("s_tbl")

    stopped = [False]

    def cut(n):
        if KSTAGE <= n:
            raise StopEmission()

    def act_table_preload(en):
        from concourse.hw_specs import get_activation_tables
        for idx, funcs in enumerate(get_activation_tables(nc.m.arch).values()):
            if AF.Exp in funcs and AF.Ln in funcs:
                return en.add_instruction(mybir.InstLoadActFuncSet(
                    name=nc.get_next_instruction_name(),
                    act_func_set_id=idx, ins=[], outs=[]))
        raise ValueError("no table set with Exp+Ln")
    b.emit("scalar", act_table_preload)

    b.dma("sync", idf[:, :], idf_d[:, :], s_msc)
    b.dma("sync", biasc[:, :], biasc_d[:, :], s_msc)
    b.dma("sync", gsb[:, :], gate2[:, :], s_msc)
    b.op("vector", lambda e: e.memset(ones_sb[:, :], 1.0), s_msc, 1)
    b.op("vector", lambda e: e.memset(stgf[0:1, 0:128], 1.0), s_msc, 1)
    b.op("vector", lambda e: e.memset(cst[:, 0:1], EPS), s_msc, 1)
    b.op("vector", lambda e: e.memset(cst[:, 1:2], -SHIFT), s_msc, 1)
    b.wait("scalar", s_msc)

    for c in range(NCH):
        b.dma("gpsimd", x_sb[:, c * TOK:(c + 1) * TOK], xin[c], s_xld)
    xgate = [16 * (c + 1) for c in range(NCH)]

    def xch(c, t0=None, t1=None):
        if t0 is None:
            return x_sb[:, c * TOK:(c + 1) * TOK]
        return x_sb[:, c * TOK + t0: c * TOK + t1]

    def emit_lora():
        b.wait("tensor", s_msc)
        b.op("tensor", lambda e: e.matmul(ps[0:128, 0:32], stgf[0:1, 0:128],
                                          gsb[0:1, :], start=True, stop=True), s_lort, 1)
        b.wait("vector", s_lort)
        b.op("vector", lambda e: e.tensor_copy(gbc[:, :], ps[0:128, 0:32]), s_lorv, 1)
        cut(1.1)

        tctr = [0]
        def transp(src_ap, dst_ap, out_rows, out_cols):
            t = tctr[0]; tctr[0] += 1
            bank_col = 2048 + (t % 2) * 512
            if t >= 2:
                b.wait("tensor", s_lorv, transp.evd[t - 2])
            b.op("tensor", lambda e, _s=src_ap, _bc=bank_col, _r=out_rows, _c=out_cols:
                 e.transpose(ps[0:_r, _bc:_bc + _c], _s, idf[0:_c, 0:_c]), s_lort, 1)
            b.wait("vector", s_lort)
            b.op("vector", lambda e, _d=dst_ap, _bc=bank_col, _r=out_rows, _c=out_cols:
                 e.tensor_copy(_d, ps[0:_r, _bc:_bc + _c]), s_lorv, 1)
            transp.evd[t] = s_lorv.n
        transp.evd = {}

        b.dma("gpsimd", stat_m[:, 0:1024], lbs.rearrange("p c r -> p (c r)"), s_lord)
        b.wait("tensor", s_lord)
        for c in range(NCH):
            transp(stat_m[:, c * 32:(c + 1) * 32],
                   b_stk[32:64, c * 128:(c + 1) * 128], 32, 128)
            b.op("vector", lambda e, _c=c: e.tensor_copy(
                b_stk[32:64, 4096 + _c * 128:4096 + (_c + 1) * 128],
                b_stk[32:64, _c * 128:(_c + 1) * 128]), s_lorv, 1)
        cut(1.2)
        for v in range(2):
            for n in range(16):
                b.wait("gpsimd", s_lorv)
                b.dma("gpsimd", rir[:, 0:1024], lb[n].rearrange("p c r -> p (c r)"), s_lord)
                b.wait("vector", s_lord)
                g = gbc[:, v * 16 + n:v * 16 + n + 1]
                if n == 0:
                    b.op("vector", lambda e, _g=g: e.tensor_scalar(
                        stat_m[:, 0:1024], rir[:, 0:1024], _g, None, ALU.mult), s_lorv, 1)
                else:
                    b.op("vector", lambda e, _g=g: e.scalar_tensor_tensor(
                        stat_m[:, 0:1024], rir[:, 0:1024], _g, stat_m[:, 0:1024],
                        ALU.mult, ALU.add), s_lorv, 1)
            b.wait("tensor", s_lorv)
            for c in range(NCH):
                transp(stat_m[:, c * 32:(c + 1) * 32],
                       b_stk[0:32, v * 4096 + c * 128:v * 4096 + (c + 1) * 128], 32, 128)
        cut(1.3)
        b.wait("gpsimd", s_lorv)
        b.dma("gpsimd", rir[0:32, 0:4096], las[:, :], s_lord)
        b.wait("tensor", s_lord)
        cut(1.32)
        for c in range(NCH):
            transp(rir[0:32, c * 128:(c + 1) * 128],
                   a_stk[:, c * 64 + 32:c * 64 + 64], 128, 32)
        cut(1.35)
        for c in range(NCH):
            b.op("vector", lambda e, _c=c: e.tensor_copy(
                a_stk[:, 2048 + _c * 64 + 32:2048 + _c * 64 + 64],
                a_stk[:, _c * 64 + 32:_c * 64 + 64]), s_lorv, 1)
        cut(1.4)
        for v in range(2):
            for n in range(16):
                for hf in range(2):
                    b.wait("gpsimd", s_lorv)
                    b.wait("gpsimd", s_lort)
                    b.dma("gpsimd", stgb[0:32, 0:2048], la[n, :, hf * 2048:(hf + 1) * 2048], s_lord)
                    b.wait("vector", s_lord)
                    g = gbc[0:32, v * 16 + n:v * 16 + n + 1]
                    acc = rir[0:32, hf * 2048:(hf + 1) * 2048]
                    if n == 0:
                        b.op("vector", lambda e, _g=g, _a=acc: e.tensor_scalar(
                            _a, stgb[0:32, 0:2048], _g, None, ALU.mult), s_lorv, 1)
                    else:
                        b.op("vector", lambda e, _g=g, _a=acc: e.scalar_tensor_tensor(
                            _a, stgb[0:32, 0:2048], _g, _a, ALU.mult, ALU.add), s_lorv, 1)
            b.wait("tensor", s_lorv)
            for c in range(NCH):
                transp(rir[0:32, c * 128:(c + 1) * 128],
                       a_stk[:, v * 2048 + c * 64:v * 2048 + c * 64 + 32], 128, 32)

    try:
        cut(1)
        emit_lora()
        cut(2)
    except StopEmission:
        stopped[0] = True

    def emit_stats(gates):
        sq0, sqf0 = s_sq.n, s_sqf.n
        def sq_chunk(c):
            b.wait("vector", s_xld, gates[c])
            if c >= 1:
                b.wait("vector", s_sqf, sqf0 + c)
            b.op("vector", lambda e, _c=c: e.tensor_tensor(
                sqb[:, :], xch(_c), xch(_c), ALU.mult), s_sq, 1)
        def mm_chunk(c):
            b.wait("tensor", s_sq, sq0 + c + 1)
            b.wait("tensor", s_xld, gates[c])
            for T in range(NT):
                b.op("tensor", lambda e, _c=c, _T=T: e.matmul(
                    ps[:, bass.ts(_T, 512)], ones_sb[:, :],
                    xch(_c, _T * TS, (_T + 1) * TS),
                    start=(_c == 0), stop=(_c == NCH - 1)), None)
            for T in range(NT):
                last = (c == NCH - 1 and T == NT - 1)
                sem = s_stm if last else (s_sqf if T == NT - 1 else None)
                b.op("tensor", lambda e, _c=c, _T=T: e.matmul(
                    ps[:, 2048 + _T * 512:2048 + (_T + 1) * 512], ones_sb[:, :],
                    sqb[:, _T * TS:(_T + 1) * TS],
                    start=(_c == 0), stop=(_c == NCH - 1)), sem, 1)
        sq_chunk(0)
        for c in range(NCH):
            mm_chunk(c)
            if c + 1 < NCH:
                sq_chunk(c + 1)
        b.wait("vector", s_stm)
        for T in range(NT):
            b.op("vector", lambda e, _T=T: e.tensor_scalar(
                stat_m[:, bass.ts(_T, 512)], ps[:, bass.ts(_T, 512)], 1.0 / DM, None, ALU.mult), None)
            b.op("vector", lambda e, _T=T: e.tensor_scalar(
                rir[:, TOK + _T * TS:TOK + (_T + 1) * TS], ps[:, 2048 + _T * 512:2048 + (_T + 1) * 512],
                1.0 / DM, None, ALU.mult), None)
        b.op("vector", lambda e: e.tensor_tensor(rir[:, 0:TOK], stat_m[:, :], stat_m[:, :], ALU.mult), None)
        b.op("vector", lambda e: e.tensor_tensor(rir[:, TOK:2 * TOK], rir[:, TOK:2 * TOK],
                                                 rir[:, 0:TOK], ALU.subtract), s_var, 1)
        b.wait("scalar", s_var)
        b.op("scalar", lambda e: e.activation(rir[:, 0:TOK], rir[:, TOK:2 * TOK], AF.Ln, bias=cst[:, 0:1]), None)
        b.op("scalar", lambda e: e.activation(rir[:, TOK:2 * TOK], rir[:, 0:TOK], AF.Exp, scale=-0.5), None)
        b.op("scalar", lambda e: e.activation(rir[:, 0:TOK], rir[:, 0:TOK], AF.Exp, scale=0.5), s_rst, 1)
        b.wait("vector", s_rst)
        b.wait("gpsimd", s_rst)
        for c in range(NCH):
            eng = "vector" if (c % 3 != 2) else "gpsimd"
            b.op(eng, lambda e, _c=c: e.tensor_tensor(xch(_c), xch(_c), stat_m[:, :], ALU.subtract), None)
            b.op(eng, lambda e, _c=c: e.tensor_tensor(xch(_c), xch(_c), rir[:, TOK:2 * TOK], ALU.mult), s_ctr, 1)

    def emit_proj(wap, P, evac, rhs=None, extra_mm=None, pre_wait=None):
        if rhs is None:
            rhs = xch
        p0w, p0m, p0e = s_wld.n, s_mmg.n, s_evd.n
        ctr_target = s_ctr.n

        def panel_dma(p):
            if p >= 2:
                b.wait("gpsimd", s_mmg, p0m + (p - 1) * NT)
            else:
                b.wait("gpsimd", s_mmg, p0m)
            par = p % 2
            b.dma("gpsimd", wpan[:, par * 4096:(par + 1) * 4096],
                  wap[p].rearrange("p c j -> p (c j)"), s_wld)

        def panel_mm(p):
            par = p % 2
            b.wait("tensor", s_wld, p0w + 16 * (p + 1))
            if p == 0:
                b.wait("tensor", s_ctr, ctr_target)
                b.wait("tensor", s_evd, p0e)
                if pre_wait is not None:
                    pre_wait()
            if p >= 2:
                b.wait("tensor", s_evd, p0e + (p - 1) * NT)
            for c in range(NCH):
                for T in range(NT):
                    bank = par * 4 + T
                    is_last = (c == NCH - 1)
                    stop = is_last and (extra_mm is None)
                    sem = s_mmg if (is_last and extra_mm is None) else None
                    b.op("tensor", lambda e, _p=p, _c=c, _T=T, _bk=bank, _par=par,
                         _stop=stop, _r=rhs(c, T * TS, (T + 1) * TS): e.matmul(
                        ps[:, bass.ts(_bk, 512)],
                        wpan[:, _par * 4096 + _c * 128:_par * 4096 + (_c + 1) * 128],
                        _r, start=(_c == 0), stop=_stop), sem, 1 if sem else 1)
            if extra_mm is not None:
                for T in range(NT):
                    extra_mm(p, T, ps[:, bass.ts(par * 4 + T, 512)])

        panel_dma(0)
        panel_dma(1)
        for p in range(P):
            panel_mm(p)
            for T in range(NT):
                evac(p, T, ps[:, bass.ts((p % 2) * 4 + T, 512)], p0m)
            if p + 2 < P:
                panel_dma(p + 2)
        return p0m

    def tile_wait(engine, p0m, p, T):
        b.wait(engine, s_mmg, p0m + p * NT + T + 1)

    def emit_ffn(gates):
        emit_stats(gates)
        p0m_ax, axe0 = s_mmg.n, s_evd.n
        b.wait("tensor", s_ctr, s_ctr.n)
        for T in range(NT):
            v = T // 2
            bank = 4 + (T % 2)
            if T >= 2:
                b.wait("tensor", s_evd, axe0 + T - 1)
            for c in range(NCH):
                sem = s_mmg if c == NCH - 1 else None
                b.op("tensor", lambda e, _c=c, _T=T, _v=v, _bk=bank: e.matmul(
                    ps[0:64, bass.ts(_bk, 512)],
                    a_stk[:, _v * 2048 + _c * 64:_v * 2048 + (_c + 1) * 64],
                    xch(_c, _T * TS, (_T + 1) * TS),
                    start=(_c == 0), stop=(_c == NCH - 1)), sem, 1)
            b.wait("vector", s_mmg, p0m_ax + T + 1)
            b.op("vector", lambda e, _T=T, _bk=bank: e.tensor_copy(
                ax_sb[0:64, bass.ts(_T, 512)], ps[0:64, bass.ts(_bk, 512)]), s_evd, 1)

        def extra(p, T, psap):
            v = T // 2
            b.op("tensor", lambda e, _p=p, _v=v, _T=T, _ps=psap: e.matmul(
                _ps, b_stk[0:64, _v * 4096 + _p * 128:_v * 4096 + (_p + 1) * 128],
                ax_sb[0:64, bass.ts(_T, 512)], start=False, stop=True), s_mmg, 1)

        stb = s_std.n

        def evac(p, T, psap, p0m):
            tile = p * NT + T
            slot = tile % 4
            fs = tile % 2
            tile_wait("vector", p0m, p, T)
            b.wait("vector", s_std, s_std.n)
            b.op("vector", lambda e, _p=p, _T=T, _f=fs: e.tensor_tensor(
                stgf[:, bass.ts(_f, 512)], xch(_p, _T * TS, (_T + 1) * TS),
                rir[:, _T * TS:(_T + 1) * TS], ALU.mult), None)
            b.op("vector", lambda e, _p=p, _T=T, _f=fs: e.scalar_tensor_tensor(
                stgf[:, bass.ts(_f, 512)], stgf[:, bass.ts(_f, 512)], biasc[:, _p:_p + 1],
                stat_m[:, _T * TS:(_T + 1) * TS], ALU.add, ALU.add), None)
            b.op("vector", lambda e, _ps=psap, _f=fs, _s=slot: e.scalar_tensor_tensor(
                stgb[:, bass.ts(_s, 512)], _ps, 1.0, stgf[:, bass.ts(_f, 512)],
                ALU.mult, ALU.add), s_evd, 1)
            b.wait("sync", s_evd)
            b.dma("sync", xnext[p, :, T * TS:(T + 1) * TS], stgb[:, bass.ts(slot, 512)], s_std)

        def pre_wait():
            b.wait("tensor", s_evd, axe0 + NT)
        emit_proj(w["wkf"], 32, evac, extra_mm=extra, pre_wait=pre_wait)

    def emit_attn(spec, gates):
        nm = spec["name"]
        H, dh, dhc, kvw = spec["H"], spec["dh"], spec["dhc"], spec["kvw"]
        QB, nslot, G, nmask = spec["QB"], spec["nslot"], spec["G"], spec["nmask"]
        scale, hper, half = spec["scale"], spec["hper"], spec["half"]
        rope_d = ropeA_d if nm == "a" else ropeB_d
        nrows = 128 if nm == "a" else 64
        mk_d = mka if nm == "a" else mkb
        kvP = kvw // 128

        emit_stats(gates)
        cut(3)
        b.wait("sync", s_ctr, s_ctr.n)
        b.dma("sync", rir[0:nrows, TOK:2 * TOK], rope_d[:, :], s_tbl)
        b.wait("vector", s_tbl, s_tbl.n)

        def rope_rows(psap, slot, T, sem=None):
            rope_rows.sem = sem
            t0 = T * TS
            f0c = bass.ts(0, 512)
            f1c = bass.ts(1, 512)
            b.op("vector", lambda e, _ps=psap: e.tensor_tensor(
                stgf[0:half, f0c], _ps[0:half, :], rir[0:half, TOK + t0:TOK + t0 + TS], ALU.mult), None)
            b.op("vector", lambda e, _ps=psap: e.tensor_tensor(
                stgf[half:2 * half, f0c], _ps[half:2 * half, :], rir[0:half, TOK + t0:TOK + t0 + TS], ALU.mult), None)
            b.op("vector", lambda e, _ps=psap: e.tensor_tensor(
                stgf[0:half, f1c], _ps[half:2 * half, :], rir[half:2 * half, TOK + t0:TOK + t0 + TS], ALU.mult), None)
            b.op("vector", lambda e, _ps=psap: e.tensor_tensor(
                stgf[half:2 * half, f1c], _ps[0:half, :], rir[half:2 * half, TOK + t0:TOK + t0 + TS], ALU.mult), None)
            b.op("vector", lambda e, _s=slot: e.tensor_tensor(
                stgb[0:half, bass.ts(_s, 512)], stgf[0:half, f0c],
                stgf[0:half, f1c], ALU.subtract), None)
            b.op("vector", lambda e, _s=slot: e.tensor_tensor(
                stgb[half:2 * half, bass.ts(_s, 512)], stgf[half:2 * half, f0c],
                stgf[half:2 * half, f1c], ALU.add), rope_rows.sem, 1)
            rope_rows.sem = None

        def emit_qkv(wname, P, kind, dst_flat):
            stb = s_std.n

            def evac(p, T, psap, p0m):
                tile = p * NT + T
                slot = tile % 4
                rope_here = (kind == "rope") and (nm == "a" or p % 2 == 0)
                eng = "scalar" if kind == "plain" else "vector"
                tile_wait(eng, p0m, p, T)
                b.wait(eng, s_std, s_std.n)
                if rope_here:
                    if nm == "b":
                        rope_rows(psap, slot, T)
                        b.op("vector", lambda e, _ps=psap, _s=slot: e.tensor_copy(
                            stgb[64:128, bass.ts(_s, 512)], _ps[64:128, :]), s_evd, 1)
                    else:
                        rope_rows(psap, slot, T, sem=s_evd)
                elif kind == "rope":
                    b.op("vector", lambda e, _ps=psap, _s=slot: e.tensor_copy(
                        stgb[:, bass.ts(_s, 512)], _ps), s_evd, 1)
                else:
                    b.op("scalar", lambda e, _ps=psap, _s=slot: e.activation(
                        stgb[:, bass.ts(_s, 512)], _ps, AF.Copy), s_evd, 1)
                b.wait("sync", s_evd)
                b.dma("sync", dst_flat[p * 128:(p + 1) * 128, T * TS:(T + 1) * TS],
                      stgb[:, bass.ts(slot, 512)], s_std)

            return emit_proj(w[wname], P, evac)

        suff = "_a" if nm == "a" else "_b"
        emit_qkv("wk" + suff, kvP, "rope", kf[nm])
        b.wait("gpsimd", s_std)
        b.op("gpsimd", lambda e: e.collective_compute(
            "AllGather", ALU.bypass, replica_groups=RG,
            ins=[kf[nm].opt()], outs=[kfag[nm].opt()]), s_ag, 1)
        vstd0 = s_std.n
        emit_qkv("wv" + suff, kvP, "plain", vf[nm].rearrange("p q t -> (p q) t"))

        b.wait("scalar", s_std)
        vb0, vbs0 = s_vb.n, s_vbs.n
        vflat = vf[nm].rearrange("p q t -> (p q) t")
        wsub = min(kvw, 1024)
        nh = kvw // wsub
        idx = 0
        for t in range(16):
            for hh in range(nh):
                par = idx % 2
                if idx >= 2:
                    b.wait("scalar", s_vbs, vbs0 + 16 * (idx - 1))
                b.dma("scalar", sqb[:, par * 1024:par * 1024 + wsub],
                      vflat[hh * wsub:(hh + 1) * wsub, t * 128:(t + 1) * 128],
                      s_vb, transpose=True)
                b.wait("scalar", s_vb, vb0 + 16 * (idx + 1))
                b.dma("scalar", vt[nm][t * 128:(t + 1) * 128, hh * wsub:(hh + 1) * wsub],
                      sqb[:, par * 1024:par * 1024 + wsub], s_vbs)
                idx += 1

        b.wait("gpsimd", s_vbs)
        b.op("gpsimd", lambda e: e.collective_compute(
            "AllGather", ALU.bypass, replica_groups=RG,
            ins=[vt[nm].opt()], outs=[vtag[nm].opt()]), s_ag, 1)
        ag_target = s_ag.n

        emit_qkv("wq" + suff, 32, "rope", qf.rearrange("p q t -> (p q) t"))

        b.wait("gpsimd", s_mmg, s_mmg.n)
        spl0 = s_spl.n
        for c in range(NCH):
            b.dma("gpsimd", xspill[c], x_sb[:, c * TOK:(c + 1) * TOK], s_spl)

        cut(4)
        QBUFW = NCH * QB
        KOFF = 2 * QBUFW
        KBUFW = dhc * 4096
        VOFF = KOFF + 2 * KBUFW
        VBUFW = 8 * 2 * 2 * dh
        OOFF = VOFF + VBUFW
        OBUFW = dhc * 512
        PTOFF = OOFF + 2 * OBUFW
        MKOFF = PTOFF + 2 * G * QB
        assert MKOFF + nmask * QB <= NCH * TOK

        kh_ap = kfag[nm].rearrange("(bb f) (j c) -> f j bb c", bb=2, c=256)
        vh_ap = vtag[nm].rearrange("(bb j h t) f -> t j bb h f", bb=2, j=8, h=2)

        nqb = TOK // QB
        srg0, sexp0, spt0, spvr0 = s_srg.n, s_exp.n, s_pt.n, s_pvr.n
        q0, ost0, rbe0 = s_q.n, s_ost.n, s_rbe.n
        regc = [0]
        headc = [0]
        blk_srg_end = {}
        kv_free = {0: 0, 1: 0}
        v_free = [0]
        mask_free = [s_pt.n]
        pend = [None]
        tail_oev = {}
        kvgroups = [0]

        def flush_tail():
            if pend[0] is None:
                return
            mh, rc_t = pend[0]
            pend[0] = None
            b.wait("tensor", s_rc, rc_t)
            if mh >= 1:
                b.wait("tensor", s_rbe, rbe0 + mh)
            b.op("tensor", lambda e: e.matmul(ps[:, 3584:3584 + QB], ones_sb[0:1, :],
                                              rc_b[0:1, 0:QB], start=True, stop=True), s_rb, 1)
            b.wait("vector", s_rb, s_rb.n)
            if mh >= 2:
                b.wait("vector", s_oev, tail_oev[mh - 2])
            b.op("vector", lambda e, _mh=mh: e.tensor_copy(
                stgf[:, (_mh % 2) * 512:(_mh % 2) * 512 + QB], ps[:, 3584:3584 + QB]), s_rbe, 1)
            opar = mh % 2
            b.wait("vector", s_ost, s_ost.n)
            for cdh in range(dhc):
                b.op("vector", lambda e, _c=cdh, _mh=mh, _op=opar: e.scalar_tensor_tensor(
                    x_sb[:, OOFF + _op * OBUFW + _c * 512: OOFF + _op * OBUFW + _c * 512 + QB],
                    ps[:, 2048 + _c * 512: 2048 + _c * 512 + QB], 1.0,
                    stgf[:, (_mh % 2) * 512:(_mh % 2) * 512 + QB], ALU.mult, ALU.mult), s_oev, 1)
            tail_oev[mh] = s_oev.n
            b.wait("sync", s_oev, s_oev.n)
            for cdh in range(dhc):
                cf = heads_cf[mh] * dhc + cdh
                i_of = heads_blk[mh]
                b.dma("sync", off_d[cf, :, i_of * QB:(i_of + 1) * QB],
                      x_sb[:, OOFF + opar * OBUFW + cdh * 512: OOFF + opar * OBUFW + cdh * 512 + QB], s_ost)

        heads_cf = {}
        heads_blk = {}

        def emit_head(i, h, nj, nsl):
            qpar = i % 2
            jkv = h // hper
            kvpar = jkv % 2
            if h % hper == 0:
                f0 = jkv * dh
                b.wait("sync", s_spl, spl0 + 16 * NCH)
                b.wait("scalar", s_spl, spl0 + 16 * NCH)
                b.wait("sync", s_ag, ag_target)
                b.wait("scalar", s_ag, ag_target)
                if kvgroups[0] >= 2:
                    b.wait("sync", s_srg, kv_free[kvpar])
                for cdh in range(dhc):
                    for bb in range(2):
                        b.dma("sync",
                              x_sb[:, KOFF + kvpar * KBUFW + cdh * 4096:
                                   KOFF + kvpar * KBUFW + cdh * 4096 + nj * 512]
                              .rearrange("p (j bb c) -> p j bb c", j=nj, bb=2)[:, :, bb, :],
                              kh_ap[f0 + cdh * 128: f0 + (cdh + 1) * 128, 0:nj, bb, :], s_kkp[kvpar])
                if kvgroups[0] >= 1:
                    b.wait("scalar", s_pvr, v_free[0])
                for bb in range(2):
                    for hh in range(2):
                        b.dma("scalar",
                              x_sb[:, VOFF: VOFF + nsl * dh]
                              .rearrange("p (j bb hh f) -> p j bb hh f", j=nj, bb=2, hh=2)[:, :, bb, hh, :],
                              vh_ap[:, 0:nj, bb, hh, f0:f0 + dh], s_vv)
                kvgroups[0] += 1
            nreg = nsl // G
            b.wait("tensor", s_q, q0 + 16 * NCH * (i + 1))
            b.wait("tensor", s_kkp[kvpar], s_kkp[kvpar].n)
            b.wait("tensor", s_vv, s_vv.n)
            tail0 = nsl - nmask

            def emit_S(r):
                greg = regc[0]
                slot = greg % 2
                if SERIAL_ATT:
                    for sm in (s_exp, s_pt, s_pvr, s_oev, s_rbe, s_rc, s_rb):
                        b.wait("tensor", sm, sm.n)
                    b.wait("scalar", s_pt, s_pt.n)
                if greg >= 2:
                    b.wait("tensor", s_exp, sexp0 + greg - 1)
                for sl in range(r * G, (r + 1) * G):
                    for cdh in range(dhc):
                        sem = s_srg if (sl == (r + 1) * G - 1 and cdh == dhc - 1) else None
                        b.op("tensor", lambda e, _sl=sl, _c=cdh, _slot=slot, _r=r: e.matmul(
                            ps[:, _slot * 1024 + (_sl - _r * G) * QB:
                               _slot * 1024 + (_sl - _r * G + 1) * QB],
                            x_sb[:, KOFF + kvpar * KBUFW + _c * 4096 + _sl * 128:
                                 KOFF + kvpar * KBUFW + _c * 4096 + (_sl + 1) * 128],
                            x_sb[:, qpar * QBUFW + (h * dhc + _c) * QB:
                                 qpar * QBUFW + (h * dhc + _c + 1) * QB],
                            start=(_c == 0), stop=(_c == dhc - 1)), sem, 1)
                b.wait("scalar", s_srg, srg0 + regc[0] + 1)
                if greg >= 2:
                    b.wait("scalar", s_pvr, spvr0 + greg - 1)
                b.op("scalar", lambda e, _slot=slot: e.activation(
                    x_sb[:, PTOFF + _slot * G * QB: PTOFF + (_slot + 1) * G * QB],
                    ps[:, _slot * 1024: _slot * 1024 + G * QB],
                    AF.Exp, bias=cst[:, 1:2], scale=scale), s_exp, 1)
                b.wait("vector", s_exp, sexp0 + regc[0] + 1)
                b.wait("vector", s_mk, s_mk.n)
                msl = [sl for sl in range(r * G, (r + 1) * G) if sl >= tail0]
                for sl in msl:
                    m = sl - tail0
                    last = (sl == msl[-1])
                    b.op("vector", lambda e, _slot=slot, _d=(sl - r * G), _m=m: e.tensor_tensor(
                        x_sb[:, PTOFF + _slot * G * QB + _d * QB: PTOFF + _slot * G * QB + (_d + 1) * QB],
                        x_sb[:, PTOFF + _slot * G * QB + _d * QB: PTOFF + _slot * G * QB + (_d + 1) * QB],
                        x_sb[:, MKOFF + _m * QB: MKOFF + (_m + 1) * QB], ALU.mult),
                        s_pt if last else None, 1)
                if not msl:
                    b.op("vector", lambda e: e.engine_nop(), s_pt, 1)
                regc[0] += 1

            def emit_PV(r):
                greg_pv = pvc[0]
                slot = greg_pv % 2
                b.wait("tensor", s_pt, spt0 + greg_pv + 1)
                if pv_first[0]:
                    b.wait("tensor", s_oev, s_oev.n)
                    pv_first[0] = False
                for sl in range(r * G, (r + 1) * G):
                    for cdh in range(dhc):
                        b.op("tensor", lambda e, _sl=sl, _c=cdh, _slot=slot, _r=r: e.matmul(
                            ps[:, 2048 + _c * 512: 2048 + _c * 512 + QB],
                            x_sb[:, VOFF + _sl * dh + _c * 128: VOFF + _sl * dh + (_c + 1) * 128],
                            x_sb[:, PTOFF + _slot * G * QB + (_sl - _r * G) * QB:
                                 PTOFF + _slot * G * QB + (_sl - _r * G + 1) * QB],
                            start=(_sl == 0), stop=(_sl == nsl - 1)), None)
                    sem = s_pvr if sl == (r + 1) * G - 1 else None
                    if QB == 256:
                        if sl % 2 == 1:
                            b.op("tensor", lambda e, _sl=sl, _slot=slot, _r=r: e.matmul(
                                ps[0:1, 3072:3072 + 2 * QB], ones_sb[:, 0:1],
                                x_sb[:, PTOFF + _slot * G * QB + (_sl - 1 - _r * G) * QB:
                                     PTOFF + _slot * G * QB + (_sl + 1 - _r * G) * QB],
                                start=(_sl == 1), stop=(_sl == nsl - 1)), sem, 1)
                        elif sem is not None:
                            b.op("tensor", lambda e: e.engine_nop(), sem, 1)
                    else:
                        b.op("tensor", lambda e, _sl=sl, _slot=slot, _r=r: e.matmul(
                            ps[0:1, 3072:3072 + QB], ones_sb[:, 0:1],
                            x_sb[:, PTOFF + _slot * G * QB + (_sl - _r * G) * QB:
                                 PTOFF + _slot * G * QB + (_sl - _r * G + 1) * QB],
                            start=(_sl == 0), stop=(_sl == nsl - 1)), sem, 1)
                pvc[0] += 1

            if SERIAL_ATT:
                flush_tail()
                for r in range(nreg):
                    emit_S(r)
                    emit_PV(r)
            else:
                for r in range(nreg):
                    emit_S(r)
                    if r == 0:
                        flush_tail()
                    if r >= 1:
                        emit_PV(r - 1)
                emit_PV(nreg - 1)

            if h % hper == hper - 1:
                kv_free[kvpar] = s_srg.n
                v_free[0] = s_pvr.n
            b.wait("scalar", s_pvr, s_pvr.n)
            if QB == 256:
                b.wait("vector", s_pvr, s_pvr.n)
                b.op("vector", lambda e: e.tensor_copy(
                    rir[0:1, TOK + 512:TOK + 768], ps[0:1, 3072 + QB:3072 + 2 * QB]), None)
                b.op("vector", lambda e: e.tensor_tensor(
                    ps[0:1, 3072:3072 + QB], ps[0:1, 3072:3072 + QB],
                    rir[0:1, TOK + 512:TOK + 768], ALU.add), s_fld, 1)
                b.wait("scalar", s_fld, s_fld.n)
            b.op("scalar", lambda e: e.activation(rc_f[0:1, 0:QB], ps[0:1, 3072:3072 + QB], AF.Ln), None)
            b.op("scalar", lambda e: e.activation(rc_b[0:1, 0:QB], rc_f[0:1, 0:QB], AF.Exp, scale=-1.0), s_rc, 1)
            mh = headc[0]
            heads_cf[mh] = h
            heads_blk[mh] = i
            pend[0] = (mh, s_rc.n)
            headc[0] += 1
            if SERIAL_ATT:
                flush_tail()

        rc_f = rir[0:1, TOK:TOK + 512]
        rc_b = sqb[0:1, 0:512]
        pvc = [0]
        pv_first = [True]
        b.wait("tensor", s_evd, s_evd.n)

        for i in range(nqb):
            nj = (SEQ // nslot) * (i + 1) // 512
            nsl = nj * 4
            qpar = i % 2
            b.wait("sync", s_spl, spl0 + 16 * NCH)
            b.wait("sync", s_ag, ag_target)
            b.wait("sync", s_std)
            if i >= 2:
                b.wait("sync", s_srg, blk_srg_end[i - 2])
            for c in range(NCH):
                b.dma("sync", x_sb[:, qpar * QBUFW + c * QB: qpar * QBUFW + (c + 1) * QB],
                      qf[c, :, i * QB:(i + 1) * QB], s_q)
            b.wait("scalar", s_spl, spl0 + 16 * NCH)
            b.wait("scalar", s_pt, mask_free[0])
            for m in range(nmask):
                b.dma("scalar", x_sb[:, MKOFF + m * QB: MKOFF + (m + 1) * QB],
                      mk_d[i, m], s_mk)
            for h in range(H):
                pv_first[0] = True
                emit_head(i, h, nj, nsl)
            blk_srg_end[i] = s_srg.n
            mask_free[0] = s_pt.n
        flush_tail()

        cut(5)
        b.wait("scalar", s_ost, s_ost.n)
        b.wait("scalar", s_rb, s_rb.n)
        b.wait("scalar", s_pvr, s_pvr.n)
        ofl0 = s_ofl.n
        for c in range(NCH):
            b.dma("scalar", x_sb[:, c * TOK:(c + 1) * TOK], off_d[c], s_ofl)

        stb2, xrr0 = s_std.n, s_xrr.n
        evd_base = [None]

        def wo_evac(p, T, psap, p0m):
            tile = p * NT + T
            slot = tile % 4
            fs = tile % 2
            xs = tile % 4
            if evd_base[0] is None:
                evd_base[0] = s_evd.n
            if tile >= 4:
                b.wait("scalar", s_evd, evd_base[0] + tile - 4 + 1)
            b.dma("scalar", sqb[:, bass.ts(xs, 512)], xspill[p, :, T * TS:(T + 1) * TS], s_xrr)
            tile_wait("vector", p0m, p, T)
            b.wait("vector", s_xrr, xrr0 + 16 * (tile + 1))
            b.wait("vector", s_std, s_std.n)
            b.op("vector", lambda e, _T=T, _xs=xs, _f=fs: e.tensor_tensor(
                stgf[:, bass.ts(_f, 512)], sqb[:, bass.ts(_xs, 512)],
                rir[:, _T * TS:(_T + 1) * TS], ALU.mult), None)
            b.op("vector", lambda e, _T=T, _f=fs: e.tensor_tensor(
                stgf[:, bass.ts(_f, 512)], stgf[:, bass.ts(_f, 512)],
                stat_m[:, _T * TS:(_T + 1) * TS], ALU.add), None)
            b.op("vector", lambda e, _ps=psap, _f=fs, _s=slot: e.scalar_tensor_tensor(
                stgb[:, bass.ts(_s, 512)], _ps, 1.0, stgf[:, bass.ts(_f, 512)],
                ALU.mult, ALU.add), s_evd, 1)
            b.wait("sync", s_evd)
            b.dma("sync", xnext[p, :, T * TS:(T + 1) * TS], stgb[:, bass.ts(slot, 512)], s_std)

        def wo_pre():
            b.wait("tensor", s_ofl, ofl0 + 16 * NCH)
            b.wait("tensor", s_oev, s_oev.n)
            b.wait("tensor", s_rbe, s_rbe.n)
        emit_proj(w["wo" + suff], 32, wo_evac, pre_wait=wo_pre)

    try:
        if stopped[0]:
            raise StopEmission()
        order = ["attn_a", "ffn", "attn_b", "ffn"][:NSUB]
        for li, kind in enumerate(order):
            if li > 0:
                b.wait("gpsimd", s_std)
                rl0 = s_xld.n
                for c in range(NCH):
                    b.dma("gpsimd", x_sb[:, c * TOK:(c + 1) * TOK], xnext[c], s_xld)
                xgate = [rl0 + 16 * (c + 1) for c in range(NCH)]
            if kind == "attn_a":
                emit_attn(ATTN_A, xgate)
            elif kind == "attn_b":
                emit_attn(ATTN_B, xgate)
            else:
                emit_ffn(xgate)
    except StopEmission:
        stopped[0] = True

    if stopped[0]:
        for sm in sems.values():
            if sm.n:
                b.wait("gpsimd", sm, sm.n)
        for c in range(NCH):
            b.dma("gpsimd", y[c], x_sb[:, c * TOK:(c + 1) * TOK], s_yst)
        b.wait("gpsimd", s_yst)
    else:
        b.wait("gpsimd", s_std)
        for c in range(NCH):
            b.dma("gpsimd", y[c], xnext[c], s_yst)
        b.wait("gpsimd", s_yst)

    with nc.Block() as block:
        @block.sync
        def _(e):
            for fn in b.q["sync"]:
                fn(e)

        @block.scalar
        def _(e):
            for fn in b.q["scalar"]:
                fn(e)

        @block.vector
        def _(e):
            for fn in b.q["vector"]:
                fn(e)

        @block.tensor
        def _(e):
            for fn in b.q["tensor"]:
                fn(e)

        @block.gpsimd
        def _(e):
            for fn in b.q["gpsimd"]:
                fn(e)


    ctx.close()
    return nc


def kernel(**inputs):
    X = np.asarray(inputs["L0_attn_a_X"], dtype=np.float32).reshape(4 * SEQ, DM)
    gate = np.asarray(inputs["L0_ffn_gate"], dtype=np.float32)
    bias = np.asarray(inputs["L0_ffn_bias"], dtype=np.float32)

    wq_a = tile_weight(np.asarray(inputs["L0_attn_a_Wq"], np.float32))
    wk_a = tile_weight(np.asarray(inputs["L0_attn_a_Wk"], np.float32))
    wv_a = tile_weight(np.asarray(inputs["L0_attn_a_Wv"], np.float32))
    wo_a = tile_weight(np.asarray(inputs["L0_attn_a_Wo"], np.float32))
    wq_b = tile_weight(rope_perm_cols(np.ascontiguousarray(np.asarray(inputs["L0_attn_b_w_q"], np.float32).T)))
    wk_b = tile_weight(rope_perm_cols(np.ascontiguousarray(np.asarray(inputs["L0_attn_b_w_k"], np.float32).T)))
    wv_b = tile_weight(np.ascontiguousarray(np.asarray(inputs["L0_attn_b_w_v"], np.float32).T))
    wo_b = tile_weight(np.ascontiguousarray(np.asarray(inputs["L0_attn_b_w_out"], np.float32).T))
    wkf = tile_weight(np.asarray(inputs["L0_ffn_kernel"], np.float32))
    la_in = np.ascontiguousarray(np.asarray(inputs["L0_ffn_lora_A"], np.float32))
    lb_in = np.ascontiguousarray(np.asarray(inputs["L0_ffn_lora_B"], np.float32)
                                 .reshape(16, 32, 128, 32).transpose(0, 2, 1, 3))
    las_in = np.ascontiguousarray(np.asarray(inputs["L0_ffn_lora_A_shared"], np.float32))
    lbs_in = np.ascontiguousarray(np.asarray(inputs["L0_ffn_lora_B_shared"], np.float32)
                                  .reshape(32, 128, 32).transpose(1, 0, 2))
    biasc_in = np.ascontiguousarray(bias.reshape(NCH, 128).T)
    idf_in = np.eye(128, dtype=np.float32)

    nc = build_program()
    in_maps = []
    for core in range(8):
        rows = core_rows(core)
        xl = np.ascontiguousarray(X[rows].T).reshape(NCH, 128, TOK)
        ca, sa, cb, sbn = make_rope_tables(core)
        ra = np.ascontiguousarray(np.vstack([ca, sa]))
        rb = np.ascontiguousarray(np.vstack([cb, sbn]))
        masks = make_masks(core)
        s = core // 2
        g2 = np.ascontiguousarray(gate[2 * s:2 * s + 2].reshape(1, 32))
        in_maps.append({
            "xin": xl, "wq_a": wq_a, "wk_a": wk_a, "wv_a": wv_a, "wo_a": wo_a,
            "wq_b": wq_b, "wk_b": wk_b, "wv_b": wv_b, "wo_b": wo_b, "wkf": wkf,
            "la": la_in, "lb": lb_in, "las": las_in, "lbs": lbs_in,
            "biasc": biasc_in, "gate2": g2,
            "ropeA": ra, "ropeB": rb,
            "mka": masks["a"], "mkb": masks["b"], "idf": idf_in,
        })
    res = run_bass_kernel_spmd(nc, in_maps, core_ids=list(range(8)))
    out = np.empty((4 * SEQ, DM), dtype=np.float32)
    for core in range(8):
        yc = res.results[core]["y"].reshape(DM, TOK)
        out[core_rows(core)] = yc.T
    return out.reshape(4, SEQ, DM)


# version 09
# speedup vs baseline: 1.1303x; 1.1303x over previous
import math
import numpy as np
import ml_dtypes
import concourse.bass as bass
import concourse.mybir as mybir
from concourse.bass_utils import run_bass_kernel_spmd

F32 = mybir.dt.float32
BF16 = mybir.dt.bfloat16
AF = mybir.ActivationFunctionType
ALU = mybir.AluOpType
BFDT = ml_dtypes.bfloat16

DM = 4096
NCH = 32
TOK = 2048
SEQ = 4096
NT = 4
TS = 512
NSUB = 4
KSTAGE = 99
SERIAL_ATT = 0
RG = [[0, 1], [2, 3], [4, 5], [6, 7]]
SHIFT = 16.0
EPS = 1e-5

ATTN_A = dict(name="a", H=32, dh=128, dhc=1, kvw=1024, QB=256, nslot=8,
              scale=1.0 / math.sqrt(128.0), G=4, nmask=4, hper=4, half=64)
ATTN_B = dict(name="b", H=16, dh=256, dhc=2, kvw=4096, QB=512, nslot=4,
              scale=1.0 / 16.0, G=2, nmask=8, hper=1, half=32)

ENGINES = ("sync", "scalar", "vector", "tensor", "gpsimd")


class Sem:
    __slots__ = ("h", "n", "name")
    def __init__(self, h, name):
        self.h, self.n, self.name = h, 0, name


class StopEmission(Exception):
    pass


class Bld:
    def __init__(self):
        self.q = {e: [] for e in ENGINES}

    def emit(self, e, fn):
        self.q[e].append(fn)

    def wait(self, e, sem, val=None):
        v = sem.n if val is None else val
        if v <= 0:
            return
        assert v <= sem.n, f"wait beyond emitted incs on {sem.name}: {v} > {sem.n}"
        self.emit(e, lambda en, _h=sem.h, _v=v: en.wait_ge(_h, _v))

    def dma(self, e, out, in_, sem, transpose=False):
        def f(en, _o=out, _i=in_, _h=sem.h, _t=transpose):
            en.dma_start(out=_o, in_=_i, transpose=_t).then_inc(_h, 16)
        self.emit(e, f)
        sem.n += 16

    def op(self, e, fn, sem=None, inc=1):
        if sem is None:
            self.emit(e, fn)
        else:
            def g(en, _f=fn, _h=sem.h, _k=inc):
                _f(en).then_inc(_h, _k)
            self.emit(e, g)
            sem.n += inc


def tile_weight(W):
    din, dout = W.shape
    P = dout // 128
    return np.ascontiguousarray(
        W.reshape(NCH, 128, P, 128).transpose(2, 1, 0, 3)).astype(np.float32)


def rope_perm_cols(Wt):
    W = Wt.copy()
    perm64 = np.concatenate([np.arange(0, 64, 2), np.arange(1, 64, 2)])
    for h in range(16):
        base = h * 256
        W[:, base:base + 64] = Wt[:, base + perm64]
    return W


def core_blocks(core):
    sigma = core % 2
    return [j for j in range(16) if j % 2 == 1 - sigma]


def core_rows(core):
    s = core // 2
    return np.concatenate([np.arange(s * SEQ + j * 256, s * SEQ + (j + 1) * 256)
                           for j in core_blocks(core)])


def make_masks(core):
    blocks = core_blocks(core)
    out = {}
    for spec in (ATTN_A, ATTN_B):
        nslot, nmask, QB = spec["nslot"], spec["nmask"], spec["QB"]
        ngroups = nmask // 4
        m = np.zeros((nslot, nmask, 128, QB), dtype=np.float32)
        for k in range(nslot):
            prefix = (SEQ // nslot) * (k + 1)
            njq = QB // 256
            qglob = np.concatenate(
                [np.arange(256) + 256 * blocks[k * njq + t] for t in range(njq)])
            j2_last = prefix // 512 - ngroups
            for g in range(ngroups):
                j2 = j2_last + g
                for bb in range(2):
                    jblk = 2 * j2 + (1 - bb)
                    for h in range(2):
                        g0 = 256 * jblk + 128 * h
                        kk = g0 + np.arange(128)
                        m[k, g * 4 + bb * 2 + h] = (kk[:, None] <= qglob[None, :])
        out[spec["name"]] = m.astype(BFDT)
    return out


def make_rope_tables(core):
    pos = (core_rows(core) % SEQ).astype(np.float64)
    inv_a = 1.0 / (10000.0 ** (np.arange(64) / 64.0))
    inv_b = 1.0 / (10000.0 ** (np.arange(0, 64, 2) / 64.0))
    ang_a = pos[None, :] * inv_a[:, None]
    ang_b = pos[None, :] * inv_b[:, None]
    return (np.cos(ang_a).astype(np.float32), np.sin(ang_a).astype(np.float32),
            np.cos(ang_b).astype(np.float32), np.sin(ang_b).astype(np.float32))


def build_program():
    import contextlib
    nc = bass.Bass("TRN2", target_bir_lowering=False, detect_race_conditions=False)

    xin = nc.dram_tensor("xin", [NCH, 128, TOK], F32, kind="ExternalInput").ap()
    w = {}
    for nm, P in (("wq_a", 32), ("wk_a", 8), ("wv_a", 8), ("wo_a", 32),
                  ("wq_b", 32), ("wk_b", 32), ("wv_b", 32), ("wo_b", 32),
                  ("wkf", 32)):
        w[nm] = nc.dram_tensor(nm, [P, 128, NCH, 128], F32, kind="ExternalInput").ap()
    la = nc.dram_tensor("la", [16, 32, DM], F32, kind="ExternalInput").ap()
    lb = nc.dram_tensor("lb", [16, 128, 32, 32], F32, kind="ExternalInput").ap()
    las = nc.dram_tensor("las", [32, DM], F32, kind="ExternalInput").ap()
    lbs = nc.dram_tensor("lbs", [128, 32, 32], F32, kind="ExternalInput").ap()
    biasc_d = nc.dram_tensor("biasc", [128, 32], F32, kind="ExternalInput").ap()
    gate2 = nc.dram_tensor("gate2", [1, 32], F32, kind="ExternalInput").ap()
    ropeA_d = nc.dram_tensor("ropeA", [128, TOK], F32, kind="ExternalInput").ap()
    ropeB_d = nc.dram_tensor("ropeB", [64, TOK], F32, kind="ExternalInput").ap()
    mka = nc.dram_tensor("mka", [8, 4, 128, 256], BF16, kind="ExternalInput").ap()
    mkb = nc.dram_tensor("mkb", [4, 8, 128, 512], BF16, kind="ExternalInput").ap()
    idf_d = nc.dram_tensor("idf", [128, 128], F32, kind="ExternalInput").ap()
    y = nc.dram_tensor("y", [NCH, 128, TOK], F32, kind="ExternalOutput").ap()

    xnext = nc.dram_tensor("xnext", [NCH, 128, TOK], BF16).ap()
    xspill = nc.dram_tensor("xspill", [NCH, 128, TOK], BF16).ap()
    qf = nc.dram_tensor("qfm", [NCH, 128, TOK], BF16).ap()
    off_d = nc.dram_tensor("offm", [NCH, 128, TOK], BF16).ap()
    kf = {"a": nc.dram_tensor("kfa", [1024, TOK], BF16).ap(),
          "b": nc.dram_tensor("kfb", [4096, TOK], BF16).ap()}
    vf = {"a": nc.dram_tensor("vfa", [8, 128, TOK], BF16).ap(),
          "b": nc.dram_tensor("vfb", [32, 128, TOK], BF16).ap()}
    vt = {"a": nc.dram_tensor("vta", [TOK, 1024], BF16).ap(),
          "b": nc.dram_tensor("vtb", [TOK, 4096], BF16).ap()}
    kfag = {"a": nc.dram_tensor("kfa_ag", [2048, TOK], BF16).ap(),
            "b": nc.dram_tensor("kfb_ag", [8192, TOK], BF16).ap()}
    vtag = {"a": nc.dram_tensor("vta_ag", [2 * TOK, 1024], BF16).ap(),
            "b": nc.dram_tensor("vtb_ag", [2 * TOK, 4096], BF16).ap()}

    b = Bld()
    ctx = contextlib.ExitStack()
    sb = lambda nm, shape, dt: ctx.enter_context(nc.sbuf_tensor(nm, shape, dt))
    x_sb = sb("x_sb", [128, NCH * TOK], BF16)
    wpan = sb("wpan", [128, 2 * 4096], BF16)
    stgf = sb("stgf", [128, 2 * TS], F32)
    stgb = sb("stgb", [128, 4 * TS], BF16)
    stat_m = sb("stat_m", [128, TOK], F32)
    rir = sb("rir", [128, 2 * TOK], F32)
    sqb = sb("sqb", [128, TOK], BF16)
    ones_sb = sb("ones_sb", [128, 128], BF16)
    idf = sb("idf_s", [128, 128], F32)
    biasc = sb("biasc_s", [128, 32], F32)
    gsb = sb("gsb", [1, 32], F32)
    gbc = sb("gbc", [128, 32], F32)
    cst = sb("cst", [128, 2], F32)
    a_stk = sb("a_stk", [128, 2 * NCH * 64], BF16)
    b_stk = sb("b_stk", [64, 2 * NCH * 128], BF16)
    ax_sb = sqb[0:64, :]
    ps = ctx.enter_context(nc.psum_tensor("ps", [128, 4096], F32))

    sems = {}
    def S(name):
        sems[name] = Sem(ctx.enter_context(nc.semaphore(name)), name)
        return sems[name]
    s_xld = S("s_xld"); s_wld = S("s_wld"); s_mmg = S("s_mmg"); s_evd = S("s_evd")
    s_std = S("s_std"); s_sq = S("s_sq"); s_sqf = S("s_sqf"); s_stm = S("s_stm")
    s_var = S("s_var"); s_rst = S("s_rst"); s_ctr = S("s_ctr"); s_spl = S("s_spl")
    s_vb = S("s_vb"); s_vbs = S("s_vbs"); s_ag = S("s_ag"); s_q = S("s_q")
    s_kv = S("s_kv"); s_srg = S("s_srg"); s_exp = S("s_exp"); s_pt = S("s_pt")
    s_pvr = S("s_pvr"); s_rc = S("s_rc"); s_rb = S("s_rb"); s_rbe = S("s_rbe")
    s_oev = S("s_oev"); s_ost = S("s_ost"); s_mk = S("s_mk"); s_xrr = S("s_xrr")
    s_lord = S("s_lord"); s_lort = S("s_lort"); s_lorv = S("s_lorv")
    s_msc = S("s_msc"); s_yst = S("s_yst"); s_vv = S("s_vv"); s_fld = S("s_fld")
    s_kkp = [S("s_kk0"), S("s_kk1")]
    s_ofl = S("s_ofl"); s_tbl = S("s_tbl")

    stopped = [False]

    def cut(n):
        if KSTAGE <= n:
            raise StopEmission()

    def act_table_preload(en):
        from concourse.hw_specs import get_activation_tables
        for idx, funcs in enumerate(get_activation_tables(nc.m.arch).values()):
            if AF.Exp in funcs and AF.Ln in funcs:
                return en.add_instruction(mybir.InstLoadActFuncSet(
                    name=nc.get_next_instruction_name(),
                    act_func_set_id=idx, ins=[], outs=[]))
        raise ValueError("no table set with Exp+Ln")
    b.emit("scalar", act_table_preload)

    b.dma("sync", idf[:, :], idf_d[:, :], s_msc)
    b.dma("sync", biasc[:, :], biasc_d[:, :], s_msc)
    b.dma("sync", gsb[:, :], gate2[:, :], s_msc)
    b.op("vector", lambda e: e.memset(ones_sb[:, :], 1.0), s_msc, 1)
    b.op("vector", lambda e: e.memset(stgf[0:1, 0:128], 1.0), s_msc, 1)
    b.op("vector", lambda e: e.memset(cst[:, 0:1], EPS), s_msc, 1)
    b.op("vector", lambda e: e.memset(cst[:, 1:2], -SHIFT), s_msc, 1)
    b.wait("scalar", s_msc)

    for c in range(NCH):
        b.dma("gpsimd", x_sb[:, c * TOK:(c + 1) * TOK], xin[c], s_xld)
    xgate = [16 * (c + 1) for c in range(NCH)]

    def xch(c, t0=None, t1=None):
        if t0 is None:
            return x_sb[:, c * TOK:(c + 1) * TOK]
        return x_sb[:, c * TOK + t0: c * TOK + t1]

    def emit_lora():
        b.wait("tensor", s_msc)
        b.op("tensor", lambda e: e.matmul(ps[0:128, 0:32], stgf[0:1, 0:128],
                                          gsb[0:1, :], start=True, stop=True), s_lort, 1)
        b.wait("vector", s_lort)
        b.op("vector", lambda e: e.tensor_copy(gbc[:, :], ps[0:128, 0:32]), s_lorv, 1)
        cut(1.1)

        tctr = [0]
        def transp(src_ap, dst_ap, out_rows, out_cols):
            t = tctr[0]; tctr[0] += 1
            bank_col = 2048 + (t % 2) * 512
            if t >= 2:
                b.wait("tensor", s_lorv, transp.evd[t - 2])
            b.op("tensor", lambda e, _s=src_ap, _bc=bank_col, _r=out_rows, _c=out_cols:
                 e.transpose(ps[0:_r, _bc:_bc + _c], _s, idf[0:_c, 0:_c]), s_lort, 1)
            b.wait("vector", s_lort)
            b.op("vector", lambda e, _d=dst_ap, _bc=bank_col, _r=out_rows, _c=out_cols:
                 e.tensor_copy(_d, ps[0:_r, _bc:_bc + _c]), s_lorv, 1)
            transp.evd[t] = s_lorv.n
        transp.evd = {}

        b.dma("gpsimd", stat_m[:, 0:1024], lbs.rearrange("p c r -> p (c r)"), s_lord)
        b.wait("tensor", s_lord)
        for c in range(NCH):
            transp(stat_m[:, c * 32:(c + 1) * 32],
                   b_stk[32:64, c * 128:(c + 1) * 128], 32, 128)
            b.op("vector", lambda e, _c=c: e.tensor_copy(
                b_stk[32:64, 4096 + _c * 128:4096 + (_c + 1) * 128],
                b_stk[32:64, _c * 128:(_c + 1) * 128]), s_lorv, 1)
        cut(1.2)
        lorv_mark = {}
        lord0b = s_lord.n
        for n in range(16):
            par = n % 2
            if n >= 2:
                b.wait("gpsimd", s_lorv, lorv_mark[n - 2])
            else:
                b.wait("gpsimd", s_lorv)
            b.dma("gpsimd", rir[:, par * 1024:(par + 1) * 1024],
                  lb[n].rearrange("p c r -> p (c r)"), s_lord)
            b.wait("vector", s_lord, lord0b + 16 * (n + 1))
            for v in range(2):
                g = gbc[:, v * 16 + n:v * 16 + n + 1]
                st = stat_m[:, v * 1024:(v + 1) * 1024]
                if n == 0:
                    b.op("vector", lambda e, _g=g, _st=st, _p=par: e.tensor_scalar(
                        _st, rir[:, _p * 1024:(_p + 1) * 1024], _g, None, ALU.mult), s_lorv, 1)
                else:
                    b.op("vector", lambda e, _g=g, _st=st, _p=par: e.scalar_tensor_tensor(
                        _st, rir[:, _p * 1024:(_p + 1) * 1024], _g, _st,
                        ALU.mult, ALU.add), s_lorv, 1)
            lorv_mark[n] = s_lorv.n
        b.wait("tensor", s_lorv)
        for v in range(2):
            for c in range(NCH):
                transp(stat_m[:, v * 1024 + c * 32:v * 1024 + (c + 1) * 32],
                       b_stk[0:32, v * 4096 + c * 128:v * 4096 + (c + 1) * 128], 32, 128)
        cut(1.3)
        b.wait("gpsimd", s_lorv)
        b.dma("gpsimd", rir[0:32, 0:4096], las[:, :], s_lord)
        b.wait("tensor", s_lord)
        cut(1.32)
        for c in range(NCH):
            transp(rir[0:32, c * 128:(c + 1) * 128],
                   a_stk[:, c * 64 + 32:c * 64 + 64], 128, 32)
        cut(1.35)
        for c in range(NCH):
            b.op("vector", lambda e, _c=c: e.tensor_copy(
                a_stk[:, 2048 + _c * 64 + 32:2048 + _c * 64 + 64],
                a_stk[:, _c * 64 + 32:_c * 64 + 64]), s_lorv, 1)
        cut(1.4)
        for hf in range(2):
            for n in range(16):
                b.wait("gpsimd", s_lorv)
                if n == 0 and hf == 0:
                    b.wait("gpsimd", s_lort)
                b.dma("gpsimd", stgb[0:32, 0:2048], la[n, :, hf * 2048:(hf + 1) * 2048], s_lord)
                b.wait("vector", s_lord)
                for v in range(2):
                    g = gbc[0:32, v * 16 + n:v * 16 + n + 1]
                    acc = (rir[0:32, hf * 2048:(hf + 1) * 2048] if v == 0
                           else stat_m[0:32, 0:2048])
                    if n == 0:
                        b.op("vector", lambda e, _g=g, _a=acc: e.tensor_scalar(
                            _a, stgb[0:32, 0:2048], _g, None, ALU.mult), s_lorv, 1)
                    else:
                        b.op("vector", lambda e, _g=g, _a=acc: e.scalar_tensor_tensor(
                            _a, stgb[0:32, 0:2048], _g, _a, ALU.mult, ALU.add), s_lorv, 1)
            b.wait("tensor", s_lorv)
            for c in range(16):
                cc = hf * 16 + c
                transp(stat_m[0:32, c * 128:(c + 1) * 128],
                       a_stk[:, 2048 + cc * 64:2048 + cc * 64 + 32], 128, 32)
        b.wait("tensor", s_lorv)
        for c in range(NCH):
            transp(rir[0:32, c * 128:(c + 1) * 128],
                   a_stk[:, c * 64:c * 64 + 32], 128, 32)

    try:
        cut(1)
        emit_lora()
        cut(2)
    except StopEmission:
        stopped[0] = True

    def emit_stats(gates):
        sq0, sqf0 = s_sq.n, s_sqf.n
        def sq_chunk(c):
            b.wait("vector", s_xld, gates[c])
            if c >= 1:
                b.wait("vector", s_sqf, sqf0 + c)
            b.op("vector", lambda e, _c=c: e.tensor_tensor(
                sqb[:, :], xch(_c), xch(_c), ALU.mult), s_sq, 1)
        def mm_chunk(c):
            b.wait("tensor", s_sq, sq0 + c + 1)
            b.wait("tensor", s_xld, gates[c])
            for T in range(NT):
                b.op("tensor", lambda e, _c=c, _T=T: e.matmul(
                    ps[:, bass.ts(_T, 512)], ones_sb[:, :],
                    xch(_c, _T * TS, (_T + 1) * TS),
                    start=(_c == 0), stop=(_c == NCH - 1)), None)
            for T in range(NT):
                last = (c == NCH - 1 and T == NT - 1)
                sem = s_stm if last else (s_sqf if T == NT - 1 else None)
                b.op("tensor", lambda e, _c=c, _T=T: e.matmul(
                    ps[:, 2048 + _T * 512:2048 + (_T + 1) * 512], ones_sb[:, :],
                    sqb[:, _T * TS:(_T + 1) * TS],
                    start=(_c == 0), stop=(_c == NCH - 1)), sem, 1)
        sq_chunk(0)
        for c in range(NCH):
            mm_chunk(c)
            if c + 1 < NCH:
                sq_chunk(c + 1)
        b.wait("vector", s_stm)
        for T in range(NT):
            b.op("vector", lambda e, _T=T: e.tensor_scalar(
                stat_m[:, bass.ts(_T, 512)], ps[:, bass.ts(_T, 512)], 1.0 / DM, None, ALU.mult), None)
            b.op("vector", lambda e, _T=T: e.tensor_scalar(
                rir[:, TOK + _T * TS:TOK + (_T + 1) * TS], ps[:, 2048 + _T * 512:2048 + (_T + 1) * 512],
                1.0 / DM, None, ALU.mult), None)
        b.op("vector", lambda e: e.tensor_tensor(rir[:, 0:TOK], stat_m[:, :], stat_m[:, :], ALU.mult), None)
        b.op("vector", lambda e: e.tensor_tensor(rir[:, TOK:2 * TOK], rir[:, TOK:2 * TOK],
                                                 rir[:, 0:TOK], ALU.subtract), s_var, 1)
        b.wait("scalar", s_var)
        b.op("scalar", lambda e: e.activation(rir[:, 0:TOK], rir[:, TOK:2 * TOK], AF.Ln, bias=cst[:, 0:1]), None)
        b.op("scalar", lambda e: e.activation(rir[:, TOK:2 * TOK], rir[:, 0:TOK], AF.Exp, scale=-0.5), None)
        b.op("scalar", lambda e: e.activation(rir[:, 0:TOK], rir[:, 0:TOK], AF.Exp, scale=0.5), s_rst, 1)
        b.wait("vector", s_rst)
        b.wait("gpsimd", s_rst)
        for c in range(NCH):
            eng = "vector" if (c % 3 != 2) else "gpsimd"
            b.op(eng, lambda e, _c=c: e.tensor_tensor(xch(_c), xch(_c), stat_m[:, :], ALU.subtract), None)
            b.op(eng, lambda e, _c=c: e.tensor_tensor(xch(_c), xch(_c), rir[:, TOK:2 * TOK], ALU.mult), s_ctr, 1)

    def emit_proj(wap, P, evac, rhs=None, extra_mm=None, pre_wait=None):
        if rhs is None:
            rhs = xch
        p0w, p0m, p0e = s_wld.n, s_mmg.n, s_evd.n
        ctr_target = s_ctr.n

        def panel_dma(p):
            if p >= 2:
                b.wait("gpsimd", s_mmg, p0m + (p - 1) * NT)
            else:
                b.wait("gpsimd", s_mmg, p0m)
            par = p % 2
            b.dma("gpsimd", wpan[:, par * 4096:(par + 1) * 4096],
                  wap[p].rearrange("p c j -> p (c j)"), s_wld)

        def panel_mm(p):
            par = p % 2
            b.wait("tensor", s_wld, p0w + 16 * (p + 1))
            if p == 0:
                b.wait("tensor", s_ctr, ctr_target)
                b.wait("tensor", s_evd, p0e)
                if pre_wait is not None:
                    pre_wait()
            if p >= 2:
                b.wait("tensor", s_evd, p0e + (p - 1) * NT)
            for c in range(NCH):
                for T in range(NT):
                    bank = par * 4 + T
                    is_last = (c == NCH - 1)
                    stop = is_last and (extra_mm is None)
                    sem = s_mmg if (is_last and extra_mm is None) else None
                    b.op("tensor", lambda e, _p=p, _c=c, _T=T, _bk=bank, _par=par,
                         _stop=stop, _r=rhs(c, T * TS, (T + 1) * TS): e.matmul(
                        ps[:, bass.ts(_bk, 512)],
                        wpan[:, _par * 4096 + _c * 128:_par * 4096 + (_c + 1) * 128],
                        _r, start=(_c == 0), stop=_stop), sem, 1 if sem else 1)
            if extra_mm is not None:
                for T in range(NT):
                    extra_mm(p, T, ps[:, bass.ts(par * 4 + T, 512)])

        panel_dma(0)
        panel_dma(1)
        for p in range(P):
            panel_mm(p)
            for T in range(NT):
                evac(p, T, ps[:, bass.ts((p % 2) * 4 + T, 512)], p0m)
            if p + 2 < P:
                panel_dma(p + 2)
        return p0m

    def tile_wait(engine, p0m, p, T):
        b.wait(engine, s_mmg, p0m + p * NT + T + 1)

    def emit_ffn(gates):
        emit_stats(gates)
        p0m_ax, axe0 = s_mmg.n, s_evd.n
        b.wait("tensor", s_ctr, s_ctr.n)
        for T in range(NT):
            v = T // 2
            bank = 4 + (T % 2)
            if T >= 2:
                b.wait("tensor", s_evd, axe0 + T - 1)
            for c in range(NCH):
                sem = s_mmg if c == NCH - 1 else None
                b.op("tensor", lambda e, _c=c, _T=T, _v=v, _bk=bank: e.matmul(
                    ps[0:64, bass.ts(_bk, 512)],
                    a_stk[:, _v * 2048 + _c * 64:_v * 2048 + (_c + 1) * 64],
                    xch(_c, _T * TS, (_T + 1) * TS),
                    start=(_c == 0), stop=(_c == NCH - 1)), sem, 1)
            b.wait("vector", s_mmg, p0m_ax + T + 1)
            b.op("vector", lambda e, _T=T, _bk=bank: e.tensor_copy(
                ax_sb[0:64, bass.ts(_T, 512)], ps[0:64, bass.ts(_bk, 512)]), s_evd, 1)

        def extra(p, T, psap):
            v = T // 2
            b.op("tensor", lambda e, _p=p, _v=v, _T=T, _ps=psap: e.matmul(
                _ps, b_stk[0:64, _v * 4096 + _p * 128:_v * 4096 + (_p + 1) * 128],
                ax_sb[0:64, bass.ts(_T, 512)], start=False, stop=True), s_mmg, 1)

        stb = s_std.n

        def evac(p, T, psap, p0m):
            tile = p * NT + T
            slot = tile % 4
            fs = tile % 2
            tile_wait("vector", p0m, p, T)
            b.wait("vector", s_std, s_std.n)
            b.op("vector", lambda e, _p=p, _T=T, _f=fs: e.tensor_tensor(
                stgf[:, bass.ts(_f, 512)], xch(_p, _T * TS, (_T + 1) * TS),
                rir[:, _T * TS:(_T + 1) * TS], ALU.mult), None)
            b.op("vector", lambda e, _p=p, _T=T, _f=fs: e.scalar_tensor_tensor(
                stgf[:, bass.ts(_f, 512)], stgf[:, bass.ts(_f, 512)], biasc[:, _p:_p + 1],
                stat_m[:, _T * TS:(_T + 1) * TS], ALU.add, ALU.add), None)
            b.op("vector", lambda e, _ps=psap, _f=fs, _s=slot: e.scalar_tensor_tensor(
                stgb[:, bass.ts(_s, 512)], _ps, 1.0, stgf[:, bass.ts(_f, 512)],
                ALU.mult, ALU.add), s_evd, 1)
            b.wait("sync", s_evd)
            b.dma("sync", xnext[p, :, T * TS:(T + 1) * TS], stgb[:, bass.ts(slot, 512)], s_std)

        def pre_wait():
            b.wait("tensor", s_evd, axe0 + NT)
        emit_proj(w["wkf"], 32, evac, extra_mm=extra, pre_wait=pre_wait)

    def emit_attn(spec, gates):
        nm = spec["name"]
        H, dh, dhc, kvw = spec["H"], spec["dh"], spec["dhc"], spec["kvw"]
        QB, nslot, G, nmask = spec["QB"], spec["nslot"], spec["G"], spec["nmask"]
        scale, hper, half = spec["scale"], spec["hper"], spec["half"]
        rope_d = ropeA_d if nm == "a" else ropeB_d
        nrows = 128 if nm == "a" else 64
        mk_d = mka if nm == "a" else mkb
        kvP = kvw // 128

        emit_stats(gates)
        cut(3)
        b.wait("sync", s_ctr, s_ctr.n)
        b.dma("sync", rir[0:nrows, TOK:2 * TOK], rope_d[:, :], s_tbl)
        b.wait("vector", s_tbl, s_tbl.n)

        b.wait("gpsimd", s_ctr, s_ctr.n)
        spl0 = s_spl.n
        for c in range(NCH):
            b.dma("gpsimd", xspill[c], x_sb[:, c * TOK:(c + 1) * TOK], s_spl)

        def rope_rows(psap, slot, T, sem=None):
            rope_rows.sem = sem
            t0 = T * TS
            f0c = bass.ts(0, 512)
            f1c = bass.ts(1, 512)
            b.op("vector", lambda e, _ps=psap: e.tensor_tensor(
                stgf[0:half, f0c], _ps[0:half, :], rir[0:half, TOK + t0:TOK + t0 + TS], ALU.mult), None)
            b.op("vector", lambda e, _ps=psap: e.tensor_tensor(
                stgf[half:2 * half, f0c], _ps[half:2 * half, :], rir[0:half, TOK + t0:TOK + t0 + TS], ALU.mult), None)
            b.op("vector", lambda e, _ps=psap: e.tensor_tensor(
                stgf[0:half, f1c], _ps[half:2 * half, :], rir[half:2 * half, TOK + t0:TOK + t0 + TS], ALU.mult), None)
            b.op("vector", lambda e, _ps=psap: e.tensor_tensor(
                stgf[half:2 * half, f1c], _ps[0:half, :], rir[half:2 * half, TOK + t0:TOK + t0 + TS], ALU.mult), None)
            b.op("vector", lambda e, _s=slot: e.tensor_tensor(
                stgb[0:half, bass.ts(_s, 512)], stgf[0:half, f0c],
                stgf[0:half, f1c], ALU.subtract), None)
            b.op("vector", lambda e, _s=slot: e.tensor_tensor(
                stgb[half:2 * half, bass.ts(_s, 512)], stgf[half:2 * half, f0c],
                stgf[half:2 * half, f1c], ALU.add), rope_rows.sem, 1)
            rope_rows.sem = None

        def emit_qkv(wname, P, kind, dst_flat):
            stb = s_std.n

            def evac(p, T, psap, p0m):
                tile = p * NT + T
                slot = tile % 4
                rope_here = (kind == "rope") and (nm == "a" or p % 2 == 0)
                eng = "scalar" if kind == "plain" else "vector"
                tile_wait(eng, p0m, p, T)
                b.wait(eng, s_std, s_std.n)
                if rope_here:
                    if nm == "b":
                        rope_rows(psap, slot, T)
                        b.op("vector", lambda e, _ps=psap, _s=slot: e.tensor_copy(
                            stgb[64:128, bass.ts(_s, 512)], _ps[64:128, :]), s_evd, 1)
                    else:
                        rope_rows(psap, slot, T, sem=s_evd)
                elif kind == "rope":
                    b.op("vector", lambda e, _ps=psap, _s=slot: e.tensor_copy(
                        stgb[:, bass.ts(_s, 512)], _ps), s_evd, 1)
                else:
                    b.op("scalar", lambda e, _ps=psap, _s=slot: e.activation(
                        stgb[:, bass.ts(_s, 512)], _ps, AF.Copy), s_evd, 1)
                b.wait("sync", s_evd)
                b.dma("sync", dst_flat[p * 128:(p + 1) * 128, T * TS:(T + 1) * TS],
                      stgb[:, bass.ts(slot, 512)], s_std)

            return emit_proj(w[wname], P, evac)

        suff = "_a" if nm == "a" else "_b"
        emit_qkv("wk" + suff, kvP, "rope", kf[nm])
        b.wait("gpsimd", s_std)
        b.op("gpsimd", lambda e: e.collective_compute(
            "AllGather", ALU.bypass, replica_groups=RG,
            ins=[kf[nm].opt()], outs=[kfag[nm].opt()]), s_ag, 1)
        vstd0 = s_std.n
        emit_qkv("wv" + suff, kvP, "plain", vf[nm].rearrange("p q t -> (p q) t"))

        b.wait("scalar", s_std)
        vb0, vbs0 = s_vb.n, s_vbs.n
        vflat = vf[nm].rearrange("p q t -> (p q) t")
        wsub = min(kvw, 1024)
        nh = kvw // wsub
        idx = 0
        for t in range(16):
            for hh in range(nh):
                par = idx % 2
                if idx >= 2:
                    b.wait("scalar", s_vbs, vbs0 + 16 * (idx - 1))
                b.dma("scalar", sqb[:, par * 1024:par * 1024 + wsub],
                      vflat[hh * wsub:(hh + 1) * wsub, t * 128:(t + 1) * 128],
                      s_vb, transpose=True)
                b.wait("scalar", s_vb, vb0 + 16 * (idx + 1))
                b.dma("scalar", vt[nm][t * 128:(t + 1) * 128, hh * wsub:(hh + 1) * wsub],
                      sqb[:, par * 1024:par * 1024 + wsub], s_vbs)
                idx += 1

        b.wait("gpsimd", s_vbs)
        b.op("gpsimd", lambda e: e.collective_compute(
            "AllGather", ALU.bypass, replica_groups=RG,
            ins=[vt[nm].opt()], outs=[vtag[nm].opt()]), s_ag, 1)
        ag_target = s_ag.n

        emit_qkv("wq" + suff, 32, "rope", qf.rearrange("p q t -> (p q) t"))

        cut(4)
        QBUFW = NCH * QB
        KOFF = 2 * QBUFW
        KBUFW = dhc * 4096
        VOFF = KOFF + 2 * KBUFW
        VBUFW = 8 * 2 * 2 * dh
        OOFF = VOFF + VBUFW
        OBUFW = dhc * 512
        PTOFF = OOFF + 2 * OBUFW
        MKOFF = PTOFF + 2 * G * QB
        assert MKOFF + nmask * QB <= NCH * TOK

        kh_ap = kfag[nm].rearrange("(bb f) (j c) -> f j bb c", bb=2, c=256)
        vh_ap = vtag[nm].rearrange("(bb j h t) f -> t j bb h f", bb=2, j=8, h=2)

        nqb = TOK // QB
        srg0, sexp0, spt0, spvr0 = s_srg.n, s_exp.n, s_pt.n, s_pvr.n
        q0, ost0, rbe0 = s_q.n, s_ost.n, s_rbe.n
        regc = [0]
        headc = [0]
        blk_srg_end = {}
        kv_free = {0: 0, 1: 0}
        v_free = [0]
        mask_free = [s_pt.n]
        pend = [None]
        tail_oev = {}
        kvgroups = [0]

        def flush_tail():
            if pend[0] is None:
                return
            mh, rc_t = pend[0]
            pend[0] = None
            b.wait("tensor", s_rc, rc_t)
            if mh >= 1:
                b.wait("tensor", s_rbe, rbe0 + mh)
            b.op("tensor", lambda e: e.matmul(ps[:, 3584:3584 + QB], ones_sb[0:1, :],
                                              rc_b[0:1, 0:QB], start=True, stop=True), s_rb, 1)
            b.wait("vector", s_rb, s_rb.n)
            if mh >= 2:
                b.wait("vector", s_oev, tail_oev[mh - 2])
            b.op("vector", lambda e, _mh=mh: e.tensor_copy(
                stgf[:, (_mh % 2) * 512:(_mh % 2) * 512 + QB], ps[:, 3584:3584 + QB]), s_rbe, 1)
            opar = mh % 2
            b.wait("vector", s_ost, s_ost.n)
            for cdh in range(dhc):
                b.op("vector", lambda e, _c=cdh, _mh=mh, _op=opar: e.scalar_tensor_tensor(
                    x_sb[:, OOFF + _op * OBUFW + _c * 512: OOFF + _op * OBUFW + _c * 512 + QB],
                    ps[:, 2048 + _c * 512: 2048 + _c * 512 + QB], 1.0,
                    stgf[:, (_mh % 2) * 512:(_mh % 2) * 512 + QB], ALU.mult, ALU.mult), s_oev, 1)
            tail_oev[mh] = s_oev.n
            b.wait("sync", s_oev, s_oev.n)
            for cdh in range(dhc):
                cf = heads_cf[mh] * dhc + cdh
                i_of = heads_blk[mh]
                b.dma("sync", off_d[cf, :, i_of * QB:(i_of + 1) * QB],
                      x_sb[:, OOFF + opar * OBUFW + cdh * 512: OOFF + opar * OBUFW + cdh * 512 + QB], s_ost)

        heads_cf = {}
        heads_blk = {}

        def emit_head(i, h, nj, nsl):
            qpar = i % 2
            jkv = h // hper
            kvpar = jkv % 2
            if h % hper == 0:
                f0 = jkv * dh
                b.wait("sync", s_spl, spl0 + 16 * NCH)
                b.wait("scalar", s_spl, spl0 + 16 * NCH)
                b.wait("sync", s_ag, ag_target)
                b.wait("scalar", s_ag, ag_target)
                if kvgroups[0] >= 2:
                    b.wait("sync", s_srg, kv_free[kvpar])
                for cdh in range(dhc):
                    for bb in range(2):
                        b.dma("sync",
                              x_sb[:, KOFF + kvpar * KBUFW + cdh * 4096:
                                   KOFF + kvpar * KBUFW + cdh * 4096 + nj * 512]
                              .rearrange("p (j bb c) -> p j bb c", j=nj, bb=2)[:, :, bb, :],
                              kh_ap[f0 + cdh * 128: f0 + (cdh + 1) * 128, 0:nj, bb, :], s_kkp[kvpar])
                if kvgroups[0] >= 1:
                    b.wait("scalar", s_pvr, v_free[0])
                for bb in range(2):
                    for hh in range(2):
                        b.dma("scalar",
                              x_sb[:, VOFF: VOFF + nsl * dh]
                              .rearrange("p (j bb hh f) -> p j bb hh f", j=nj, bb=2, hh=2)[:, :, bb, hh, :],
                              vh_ap[:, 0:nj, bb, hh, f0:f0 + dh], s_vv)
                kvgroups[0] += 1
            nreg = nsl // G
            b.wait("tensor", s_q, q0 + 16 * NCH * (i + 1))
            b.wait("tensor", s_kkp[kvpar], s_kkp[kvpar].n)
            b.wait("tensor", s_vv, s_vv.n)
            tail0 = nsl - nmask

            def emit_S(r):
                greg = regc[0]
                slot = greg % 2
                if SERIAL_ATT:
                    for sm in (s_exp, s_pt, s_pvr, s_oev, s_rbe, s_rc, s_rb):
                        b.wait("tensor", sm, sm.n)
                    b.wait("scalar", s_pt, s_pt.n)
                if greg >= 2:
                    b.wait("tensor", s_exp, sexp0 + greg - 1)
                for sl in range(r * G, (r + 1) * G):
                    for cdh in range(dhc):
                        sem = s_srg if (sl == (r + 1) * G - 1 and cdh == dhc - 1) else None
                        b.op("tensor", lambda e, _sl=sl, _c=cdh, _slot=slot, _r=r: e.matmul(
                            ps[:, _slot * 1024 + (_sl - _r * G) * QB:
                               _slot * 1024 + (_sl - _r * G + 1) * QB],
                            x_sb[:, KOFF + kvpar * KBUFW + _c * 4096 + _sl * 128:
                                 KOFF + kvpar * KBUFW + _c * 4096 + (_sl + 1) * 128],
                            x_sb[:, qpar * QBUFW + (h * dhc + _c) * QB:
                                 qpar * QBUFW + (h * dhc + _c + 1) * QB],
                            start=(_c == 0), stop=(_c == dhc - 1)), sem, 1)
                b.wait("scalar", s_srg, srg0 + regc[0] + 1)
                if greg >= 2:
                    b.wait("scalar", s_pvr, spvr0 + greg - 1)
                b.op("scalar", lambda e, _slot=slot: e.activation(
                    x_sb[:, PTOFF + _slot * G * QB: PTOFF + (_slot + 1) * G * QB],
                    ps[:, _slot * 1024: _slot * 1024 + G * QB],
                    AF.Exp, bias=cst[:, 1:2], scale=scale), s_exp, 1)
                b.wait("vector", s_exp, sexp0 + regc[0] + 1)
                b.wait("vector", s_mk, s_mk.n)
                msl = [sl for sl in range(r * G, (r + 1) * G) if sl >= tail0]
                for sl in msl:
                    m = sl - tail0
                    last = (sl == msl[-1])
                    b.op("vector", lambda e, _slot=slot, _d=(sl - r * G), _m=m: e.tensor_tensor(
                        x_sb[:, PTOFF + _slot * G * QB + _d * QB: PTOFF + _slot * G * QB + (_d + 1) * QB],
                        x_sb[:, PTOFF + _slot * G * QB + _d * QB: PTOFF + _slot * G * QB + (_d + 1) * QB],
                        x_sb[:, MKOFF + _m * QB: MKOFF + (_m + 1) * QB], ALU.mult),
                        s_pt if last else None, 1)
                if not msl:
                    b.op("vector", lambda e: e.engine_nop(), s_pt, 1)
                regc[0] += 1

            def emit_PV(r):
                greg_pv = pvc[0]
                slot = greg_pv % 2
                b.wait("tensor", s_pt, spt0 + greg_pv + 1)
                if pv_first[0]:
                    b.wait("tensor", s_oev, s_oev.n)
                    pv_first[0] = False
                for sl in range(r * G, (r + 1) * G):
                    for cdh in range(dhc):
                        b.op("tensor", lambda e, _sl=sl, _c=cdh, _slot=slot, _r=r: e.matmul(
                            ps[:, 2048 + _c * 512: 2048 + _c * 512 + QB],
                            x_sb[:, VOFF + _sl * dh + _c * 128: VOFF + _sl * dh + (_c + 1) * 128],
                            x_sb[:, PTOFF + _slot * G * QB + (_sl - _r * G) * QB:
                                 PTOFF + _slot * G * QB + (_sl - _r * G + 1) * QB],
                            start=(_sl == 0), stop=(_sl == nsl - 1)), None)
                    sem = s_pvr if sl == (r + 1) * G - 1 else None
                    if QB == 256:
                        if sl % 2 == 1:
                            b.op("tensor", lambda e, _sl=sl, _slot=slot, _r=r: e.matmul(
                                ps[0:1, 3072:3072 + 2 * QB], ones_sb[:, 0:1],
                                x_sb[:, PTOFF + _slot * G * QB + (_sl - 1 - _r * G) * QB:
                                     PTOFF + _slot * G * QB + (_sl + 1 - _r * G) * QB],
                                start=(_sl == 1), stop=(_sl == nsl - 1)), sem, 1)
                        elif sem is not None:
                            b.op("tensor", lambda e: e.engine_nop(), sem, 1)
                    else:
                        b.op("tensor", lambda e, _sl=sl, _slot=slot, _r=r: e.matmul(
                            ps[0:1, 3072:3072 + QB], ones_sb[:, 0:1],
                            x_sb[:, PTOFF + _slot * G * QB + (_sl - _r * G) * QB:
                                 PTOFF + _slot * G * QB + (_sl - _r * G + 1) * QB],
                            start=(_sl == 0), stop=(_sl == nsl - 1)), sem, 1)
                pvc[0] += 1

            if SERIAL_ATT:
                flush_tail()
                for r in range(nreg):
                    emit_S(r)
                    emit_PV(r)
            else:
                for r in range(nreg):
                    emit_S(r)
                    if r == 0:
                        flush_tail()
                    if r >= 1:
                        emit_PV(r - 1)
                emit_PV(nreg - 1)

            if h % hper == hper - 1:
                kv_free[kvpar] = s_srg.n
                v_free[0] = s_pvr.n
            b.wait("scalar", s_pvr, s_pvr.n)
            if QB == 256:
                b.wait("vector", s_pvr, s_pvr.n)
                b.op("vector", lambda e: e.tensor_copy(
                    rir[0:1, TOK + 512:TOK + 768], ps[0:1, 3072 + QB:3072 + 2 * QB]), None)
                b.op("vector", lambda e: e.tensor_tensor(
                    ps[0:1, 3072:3072 + QB], ps[0:1, 3072:3072 + QB],
                    rir[0:1, TOK + 512:TOK + 768], ALU.add), s_fld, 1)
                b.wait("scalar", s_fld, s_fld.n)
            b.op("scalar", lambda e: e.activation(rc_f[0:1, 0:QB], ps[0:1, 3072:3072 + QB], AF.Ln), None)
            b.op("scalar", lambda e: e.activation(rc_b[0:1, 0:QB], rc_f[0:1, 0:QB], AF.Exp, scale=-1.0), s_rc, 1)
            mh = headc[0]
            heads_cf[mh] = h
            heads_blk[mh] = i
            pend[0] = (mh, s_rc.n)
            headc[0] += 1
            if SERIAL_ATT:
                flush_tail()

        rc_f = rir[0:1, TOK:TOK + 512]
        rc_b = sqb[0:1, 0:512]
        pvc = [0]
        pv_first = [True]
        b.wait("tensor", s_evd, s_evd.n)
        b.wait("scalar", s_mmg, s_mmg.n)

        for i in range(nqb):
            nj = (SEQ // nslot) * (i + 1) // 512
            nsl = nj * 4
            qpar = i % 2
            b.wait("sync", s_spl, spl0 + 16 * NCH)
            b.wait("sync", s_ag, ag_target)
            b.wait("sync", s_std)
            if i >= 2:
                b.wait("sync", s_srg, blk_srg_end[i - 2])
            for c in range(NCH):
                b.dma("sync", x_sb[:, qpar * QBUFW + c * QB: qpar * QBUFW + (c + 1) * QB],
                      qf[c, :, i * QB:(i + 1) * QB], s_q)
            b.wait("scalar", s_spl, spl0 + 16 * NCH)
            b.wait("scalar", s_pt, mask_free[0])
            for m in range(nmask):
                b.dma("scalar", x_sb[:, MKOFF + m * QB: MKOFF + (m + 1) * QB],
                      mk_d[i, m], s_mk)
            for h in range(H):
                pv_first[0] = True
                emit_head(i, h, nj, nsl)
            blk_srg_end[i] = s_srg.n
            mask_free[0] = s_pt.n
        flush_tail()

        cut(5)
        b.wait("scalar", s_ost, s_ost.n)
        b.wait("scalar", s_rb, s_rb.n)
        b.wait("scalar", s_pvr, s_pvr.n)
        ofl0 = s_ofl.n
        for c in range(NCH):
            b.dma("scalar", x_sb[:, c * TOK:(c + 1) * TOK], off_d[c], s_ofl)

        stb2, xrr0 = s_std.n, s_xrr.n
        evd_base = [None]

        def wo_evac(p, T, psap, p0m):
            tile = p * NT + T
            slot = tile % 4
            fs = tile % 2
            xs = tile % 4
            if evd_base[0] is None:
                evd_base[0] = s_evd.n
            if tile >= 4:
                b.wait("scalar", s_evd, evd_base[0] + tile - 4 + 1)
            b.dma("scalar", sqb[:, bass.ts(xs, 512)], xspill[p, :, T * TS:(T + 1) * TS], s_xrr)
            tile_wait("vector", p0m, p, T)
            b.wait("vector", s_xrr, xrr0 + 16 * (tile + 1))
            b.wait("vector", s_std, s_std.n)
            b.op("vector", lambda e, _T=T, _xs=xs, _f=fs: e.tensor_tensor(
                stgf[:, bass.ts(_f, 512)], sqb[:, bass.ts(_xs, 512)],
                rir[:, _T * TS:(_T + 1) * TS], ALU.mult), None)
            b.op("vector", lambda e, _T=T, _f=fs: e.tensor_tensor(
                stgf[:, bass.ts(_f, 512)], stgf[:, bass.ts(_f, 512)],
                stat_m[:, _T * TS:(_T + 1) * TS], ALU.add), None)
            b.op("vector", lambda e, _ps=psap, _f=fs, _s=slot: e.scalar_tensor_tensor(
                stgb[:, bass.ts(_s, 512)], _ps, 1.0, stgf[:, bass.ts(_f, 512)],
                ALU.mult, ALU.add), s_evd, 1)
            b.wait("sync", s_evd)
            b.dma("sync", xnext[p, :, T * TS:(T + 1) * TS], stgb[:, bass.ts(slot, 512)], s_std)

        def wo_pre():
            b.wait("tensor", s_ofl, ofl0 + 16 * NCH)
            b.wait("tensor", s_oev, s_oev.n)
            b.wait("tensor", s_rbe, s_rbe.n)
        emit_proj(w["wo" + suff], 32, wo_evac, pre_wait=wo_pre)

    try:
        if stopped[0]:
            raise StopEmission()
        order = ["attn_a", "ffn", "attn_b", "ffn"][:NSUB]
        for li, kind in enumerate(order):
            if li > 0:
                b.wait("gpsimd", s_std)
                rl0 = s_xld.n
                for c in range(NCH):
                    b.dma("gpsimd", x_sb[:, c * TOK:(c + 1) * TOK], xnext[c], s_xld)
                xgate = [rl0 + 16 * (c + 1) for c in range(NCH)]
            if kind == "attn_a":
                emit_attn(ATTN_A, xgate)
            elif kind == "attn_b":
                emit_attn(ATTN_B, xgate)
            else:
                emit_ffn(xgate)
    except StopEmission:
        stopped[0] = True

    if stopped[0]:
        for sm in sems.values():
            if sm.n:
                b.wait("gpsimd", sm, sm.n)
        for c in range(NCH):
            b.dma("gpsimd", y[c], x_sb[:, c * TOK:(c + 1) * TOK], s_yst)
        b.wait("gpsimd", s_yst)
    else:
        b.wait("gpsimd", s_std)
        for c in range(NCH):
            b.dma("gpsimd", y[c], xnext[c], s_yst)
        b.wait("gpsimd", s_yst)

    with nc.Block() as block:
        @block.sync
        def _(e):
            for fn in b.q["sync"]:
                fn(e)

        @block.scalar
        def _(e):
            for fn in b.q["scalar"]:
                fn(e)

        @block.vector
        def _(e):
            for fn in b.q["vector"]:
                fn(e)

        @block.tensor
        def _(e):
            for fn in b.q["tensor"]:
                fn(e)

        @block.gpsimd
        def _(e):
            for fn in b.q["gpsimd"]:
                fn(e)


    ctx.close()
    return nc


def kernel(**inputs):
    X = np.asarray(inputs["L0_attn_a_X"], dtype=np.float32).reshape(4 * SEQ, DM)
    gate = np.asarray(inputs["L0_ffn_gate"], dtype=np.float32)
    bias = np.asarray(inputs["L0_ffn_bias"], dtype=np.float32)

    wq_a = tile_weight(np.asarray(inputs["L0_attn_a_Wq"], np.float32))
    wk_a = tile_weight(np.asarray(inputs["L0_attn_a_Wk"], np.float32))
    wv_a = tile_weight(np.asarray(inputs["L0_attn_a_Wv"], np.float32))
    wo_a = tile_weight(np.asarray(inputs["L0_attn_a_Wo"], np.float32))
    wq_b = tile_weight(rope_perm_cols(np.ascontiguousarray(np.asarray(inputs["L0_attn_b_w_q"], np.float32).T)))
    wk_b = tile_weight(rope_perm_cols(np.ascontiguousarray(np.asarray(inputs["L0_attn_b_w_k"], np.float32).T)))
    wv_b = tile_weight(np.ascontiguousarray(np.asarray(inputs["L0_attn_b_w_v"], np.float32).T))
    wo_b = tile_weight(np.ascontiguousarray(np.asarray(inputs["L0_attn_b_w_out"], np.float32).T))
    wkf = tile_weight(np.asarray(inputs["L0_ffn_kernel"], np.float32))
    la_in = np.ascontiguousarray(np.asarray(inputs["L0_ffn_lora_A"], np.float32))
    lb_in = np.ascontiguousarray(np.asarray(inputs["L0_ffn_lora_B"], np.float32)
                                 .reshape(16, 32, 128, 32).transpose(0, 2, 1, 3))
    las_in = np.ascontiguousarray(np.asarray(inputs["L0_ffn_lora_A_shared"], np.float32))
    lbs_in = np.ascontiguousarray(np.asarray(inputs["L0_ffn_lora_B_shared"], np.float32)
                                  .reshape(32, 128, 32).transpose(1, 0, 2))
    biasc_in = np.ascontiguousarray(bias.reshape(NCH, 128).T)
    idf_in = np.eye(128, dtype=np.float32)

    nc = build_program()
    in_maps = []
    for core in range(8):
        rows = core_rows(core)
        xl = np.ascontiguousarray(X[rows].T).reshape(NCH, 128, TOK)
        ca, sa, cb, sbn = make_rope_tables(core)
        ra = np.ascontiguousarray(np.vstack([ca, sa]))
        rb = np.ascontiguousarray(np.vstack([cb, sbn]))
        masks = make_masks(core)
        s = core // 2
        g2 = np.ascontiguousarray(gate[2 * s:2 * s + 2].reshape(1, 32))
        in_maps.append({
            "xin": xl, "wq_a": wq_a, "wk_a": wk_a, "wv_a": wv_a, "wo_a": wo_a,
            "wq_b": wq_b, "wk_b": wk_b, "wv_b": wv_b, "wo_b": wo_b, "wkf": wkf,
            "la": la_in, "lb": lb_in, "las": las_in, "lbs": lbs_in,
            "biasc": biasc_in, "gate2": g2,
            "ropeA": ra, "ropeB": rb,
            "mka": masks["a"], "mkb": masks["b"], "idf": idf_in,
        })
    res = run_bass_kernel_spmd(nc, in_maps, core_ids=list(range(8)))
    out = np.empty((4 * SEQ, DM), dtype=np.float32)
    for core in range(8):
        yc = res.results[core]["y"].reshape(DM, TOK)
        out[core_rows(core)] = yc.T
    return out.reshape(4, SEQ, DM)
